# version 12
# baseline (speedup 1.0000x reference)
import numpy as np
import ml_dtypes
from contextlib import ExitStack

import concourse.bass as bass
import concourse.mybir as mybir
from concourse.bass_utils import run_bass_kernel_spmd

F32 = mybir.dt.float32
BF16 = mybir.dt.bfloat16
AF = mybir.ActivationFunctionType
ALU = mybir.AluOpType
AX = mybir.AxisListType

D = 2048
DT = 16
NH = 16
DH = 128
E = 8
HID = 1792
HT = 14
WIN = 512
NP = 16
NEG = -1e30
LN_EPS = 1e-5
RMS_EPS = 1e-6
SCALE0 = DH ** -0.5
SCALE1 = 1.0 / np.sqrt(2048.0)

ENGINES = ("pe", "dve", "act", "pool", "sp")
QUEUES = {"spq": "sp", "actq": "act", "poolq": "pool"}
R_SEMS = 8


def _apkey(ap):
    t = getattr(ap, "tensor", None)
    return t.name if t is not None else ap.name


class Prog:

    def __init__(self, nc):
        self.nc = nc
        self.ops = {e: [] for e in ENGINES}
        self.count = {e: 0 for e in ENGINES}
        self.qsem_count = {}
        self.qsem_rr = {q: 0 for q in QUEUES}
        self.doms = list(ENGINES) + [f"{q}{i}" for q in QUEUES for i in range(R_SEMS)]
        self.water = {e: {d: 0 for d in self.doms} for e in ENGINES}
        for q in QUEUES:
            for i in range(R_SEMS):
                self.qsem_count[f"{q}{i}"] = 0
        self.last_w = {}
        self.last_r = {}
        self.sems = {}
        self.n_mm = 0
        self.marks = []

    def mark(self, name):
        self.marks.append((name, self.n_mm))

    def init_sems(self, ctx):
        for d in self.doms:
            self.sems[d] = ctx.enter_context(self.nc.semaphore("sem_" + d))

    def _need(self, deps, dom, val):
        if val > 0:
            deps[dom] = max(deps.get(dom, 0), val)

    def _deps_for(self, reads, writes, extra_deps):
        deps = {}
        for ap in reads:
            w = self.last_w.get(_apkey(ap))
            if w:
                self._need(deps, w[0], w[1])
        for ap in writes:
            k = _apkey(ap)
            w = self.last_w.get(k)
            if w:
                self._need(deps, w[0], w[1])
            for d2, v in self.last_r.get(k, {}).items():
                self._need(deps, d2, v)
        for (d2, v) in extra_deps:
            self._need(deps, d2, v)
        return deps

    def _record(self, eng, dom, emit, reads, writes, deps, inc, tag):
        waits = []
        for d2, v in deps.items():
            if self.water[eng][d2] < v:
                waits.append((d2, v))
                self.water[eng][d2] = v
        if dom in self.qsem_count:
            myval = self.qsem_count[dom] + inc
            self.qsem_count[dom] = myval
        else:
            myval = self.count[dom] + inc
            self.count[dom] = myval
        self.ops[eng].append((waits, emit, dom, inc, tag))
        for ap in writes:
            k = _apkey(ap)
            self.last_w[k] = (dom, myval)
            self.last_r[k] = {}
        for ap in reads:
            d = self.last_r.setdefault(_apkey(ap), {})
            d[dom] = max(d.get(dom, 0), myval)
        return myval

    def op(self, eng, emit, reads=(), writes=(), tag="", extra_deps=()):
        deps = self._deps_for(reads, writes, extra_deps)
        return self._record(eng, eng, emit, reads, writes, deps, 1, tag)

    def dma(self, queue, dst, src, tag="dma", **kw):
        queue = {"sp": "spq", "act": "actq", "pool": "poolq"}.get(queue, queue)
        eng = QUEUES[queue]
        i = self.qsem_rr[queue]
        self.qsem_rr[queue] = (i + 1) % R_SEMS
        dom = f"{queue}{i}"
        deps = self._deps_for([src], [dst], ())
        prior = self.qsem_count[dom]
        if prior:
            self._need(deps, dom, prior)
        emit = lambda e: e.dma_start(dst, src, **kw)
        return self._record(eng, dom, emit, [src], [dst], deps, 16, tag)

    def barrier(self, skip_prefix=None):
        snap = dict(self.count)
        snap.update(self.qsem_count)
        doms = [d for d in self.doms
                if skip_prefix is None or not d.startswith(skip_prefix)]
        for e in ENGINES:
            waits = []
            for d2 in doms:
                if d2 != e and snap[d2] > self.water[e][d2]:
                    waits.append((d2, snap[d2]))
            if waits:
                self.ops[e].append((waits, False, e, 0, "barrier"))
            for d2 in doms:
                self.water[e][d2] = max(self.water[e][d2], snap[d2])
        if skip_prefix is None:
            self.last_w = {}
            self.last_r = {}
        else:
            self.last_w = {k: v for k, v in self.last_w.items()
                           if v[0].startswith(skip_prefix)}
            self.last_r = {k: {d: v for d, v in rd.items()
                               if d.startswith(skip_prefix)}
                           for k, rd in self.last_r.items()}
            self.last_r = {k: rd for k, rd in self.last_r.items() if rd}

    def emit(self):
        nc = self.nc
        sems = self.sems
        with nc.Block() as blk:
            def run(eng_obj, eng_name):
                for waits, emit, dom, inc, tag in self.ops[eng_name]:
                    for (d2, v) in waits:
                        eng_obj.wait_ge(sems[d2], v)
                    if emit is False:
                        continue
                    emit(eng_obj).then_inc(sems[dom], inc)
            if self.ops["sp"]:
                blk.sync(lambda e: run(e, "sp"))
            if self.ops["pe"]:
                blk.tensor(lambda e: run(e, "pe"))
            if self.ops["dve"]:
                blk.vector(lambda e: run(e, "dve"))
            if self.ops["act"]:
                blk.scalar(lambda e: run(e, "act"))
            if self.ops["pool"]:
                blk.gpsimd(lambda e: run(e, "pool"))

    def mmgroup(self, mms, reads, writes, tag="mm"):
        self.n_mm += len(mms) + sum(1 for m in mms if False)
        def emit(e):
            last = None
            for mm in mms:
                out, lh, rh, st, sp = mm[:5]
                last = e.matmul(out, lh, rh, start=st, stop=sp)
            return last
        return self.op("pe", emit, reads=reads, writes=writes, tag=tag)

    def tt(self, eng, out, a, b, alu, tag="tt"):
        return self.op(eng, lambda e: e.tensor_tensor(out, a, b, alu),
                       reads=[a, b], writes=[out], tag=tag)

    def ts(self, eng, out, a, s1, alu, s2=None, alu2=None, tag="ts"):
        sr = [x for x in (s1, s2) if hasattr(x, "tensor")]
        if alu2 is None:
            return self.op(eng, lambda e: e.tensor_single_scalar(out, a, s1, alu),
                           reads=[a] + sr, writes=[out], tag=tag)
        return self.op(eng, lambda e: e.tensor_scalar(out, a, s1, s2, alu, alu2),
                       reads=[a] + sr, writes=[out], tag=tag)

    def stt(self, eng, out, in0, scal, in1, alu0, alu1, tag="stt"):
        sr = [scal] if hasattr(scal, "tensor") else []
        return self.op(eng, lambda e: e.scalar_tensor_tensor(out, in0, scal, in1, alu0, alu1),
                       reads=[in0, in1] + sr, writes=[out], tag=tag)

    def act(self, out, in_, func, bias=0.0, scale=1.0, accum=None, tag="act"):
        br = [bias] if hasattr(bias, "tensor") else []
        sr = [scale] if hasattr(scale, "tensor") else []
        wr = [out] + ([accum] if accum is not None else [])
        if accum is None:
            return self.op("act", lambda e: e.activation(out, in_, func, bias=bias, scale=scale),
                           reads=[in_] + br + sr, writes=wr, tag=tag)
        return self.op("act", lambda e: e.activation(out, in_, func, bias=bias, scale=scale,
                                                     accum_out=accum),
                       reads=[in_] + br + sr, writes=wr, tag=tag)

    def copy(self, eng, out, in_, tag="cp"):
        if eng == "act":
            return self.op("act", lambda e: e.copy(out, in_), reads=[in_], writes=[out], tag=tag)
        return self.op(eng, lambda e: e.tensor_copy(out, in_), reads=[in_], writes=[out], tag=tag)

    def red(self, out, in_, alu, negate=False, tag="red"):
        return self.op("dve", lambda e: e.tensor_reduce(out, in_, AX.X, alu, negate=negate),
                       reads=[in_], writes=[out], tag=tag)

    def recip(self, out, in_, tag="recip"):
        return self.op("dve", lambda e: e.reciprocal(out, in_), reads=[in_], writes=[out], tag=tag)

    def transp_mark(self):
        pass

    def transp(self, out_ps, in_sb, ident, tag="tr"):
        self.n_mm += 1
        return self.op("pe", lambda e: e.transpose(out_ps, in_sb, ident),
                       reads=[in_sb, ident], writes=[out_ps], tag=tag)



class Ctx:
    pass


def declare_io(nc, T):
    c = Ctx()
    c.T = T
    di = lambda n, s, dt=F32: nc.dram_tensor(n, s, dt, kind="ExternalInput").ap()
    ds = lambda n, s, dt: nc.dram_tensor(n, s, dt, kind="Internal").ap()
    c.xT = di("xT", [D, T])
    c.wqkv0T = di("wqkv0T", [6, 128, DT, 1024])
    c.wout0T = di("wout0T", [128, DT, D])
    c.pm0T = di("pm0T", [NH, DH, NP])
    c.pm1 = di("pm1", [NH, NP, DH])
    c.anw0c = di("anw0c", [128, DT])
    c.rw0 = di("rw0", [D, E])
    c.rb0 = di("rb0", [1, E])
    c.ew0 = di("ew0", [E, 4, 128, DT, 512])
    c.eb0c = di("eb0c", [128, E * DT])
    c.pw0 = di("pw0", [4, 128, DT, 512])
    c.pb0c = di("pb0c", [128, DT])
    c.fnw0c = di("fnw0c", [128, DT])
    c.wqkv1T = di("wqkv1T", [6, 128, DT, 1024])
    c.bqk1c = di("bqk1c", [128, 2 * DT])
    c.bv1 = di("bv1", [1, D])
    c.wout1T = di("wout1T", [128, DT, D])
    c.bo1c = di("bo1c", [128, DT])
    c.gw1 = di("gw1", [D, E])
    c.ebias1 = di("ebias1", [1, E])
    c.w11 = di("w11", [E, 7, 128, DT, 256])
    c.w21 = di("w21", [E, 2, 128, 7, D])
    c.w31 = di("w31", [E, 7, 128, DT, 256])
    c.sw1 = di("sw1", [7, 128, DT, 256])
    c.sw2 = di("sw2", [2, 128, 7, D])
    c.sw3 = di("sw3", [7, 128, DT, 256])
    c.identf = di("identf", [128, 128])
    c.identb = di("identb", [128, 128], BF16)
    c.swapT = di("swapT", [128, 128], BF16)
    c.cosT = di("cosT", [128, T], BF16)
    c.sinT = di("sinT", [128, T], BF16)
    c.maskW = di("maskW", [4, 128, WIN + NP], BF16)
    c.pmk_bf = ds("pmk_bf", [D, NP], BF16)
    c.wqkv0T_bf = ds("wqkv0T_bf", [6, 128, DT, 1024], BF16)
    c.wout0T_bf = ds("wout0T_bf", [128, DT, D], BF16)
    c.ew0_bf = ds("ew0_bf", [E, 4, 128, DT, 512], BF16)
    c.pw0_bf = ds("pw0_bf", [4, 128, DT, 512], BF16)
    c.wqkv1T_bf = ds("wqkv1T_bf", [6, 128, DT, 1024], BF16)
    c.wout1T_bf = ds("wout1T_bf", [128, DT, D], BF16)
    c.w11_bf = ds("w11_bf", [E, 7, 128, DT, 256], BF16)
    c.w21_bf = ds("w21_bf", [E, 2, 128, 7, D], BF16)
    c.w31_bf = ds("w31_bf", [E, 7, 128, DT, 256], BF16)
    c.sw1_bf = ds("sw1_bf", [7, 128, DT, 256], BF16)
    c.sw2_bf = ds("sw2_bf", [2, 128, 7, D], BF16)
    c.sw3_bf = ds("sw3_bf", [7, 128, DT, 256], BF16)
    c.xn0T = ds("xn0T", [D, T], BF16)
    c.xa0T = ds("xa0T", [D, T], BF16)
    c.q0T = ds("q0T", [D, T], BF16)
    c.k0T = ds("k0T", [D, T], BF16)
    c.v0 = ds("v0", [T, D], BF16)
    c.at0T = ds("at0T", [D, T], BF16)
    c.w0rT = ds("w0rT", [E, T], F32)
    c.x1T = ds("x1T", [D, T], F32)
    c.xn1T = ds("xn1T", [D, T], BF16)
    c.s1rT = ds("s1rT", [E, T], F32)
    c.q1T = ds("q1T", [D, T], BF16)
    c.k1T = ds("k1T", [D, T], BF16)
    c.v1 = ds("v1", [T, D], BF16)
    c.ctx1T = ds("ctx1T", [D, T], BF16)
    c.at1T = ds("at1T", [D, T], BF16)
    c.outT = nc.dram_tensor("outT", [D, T], F32, kind="ExternalOutput").ap()
    return c


def load_persistent(nc, p, c, ctx):
    sb = lambda n, s, dt=F32: ctx.enter_context(nc.sbuf_tensor(n, s, dt))
    c.identf_sb = sb("identf_sb", [128, 128])
    c.identb_sb = sb("identb_sb", [128, 128], BF16)
    c.swap_sb = sb("swap_sb", [128, 128], BF16)
    c.onesf_col = sb("onesf_col", [128, 1])
    c.onesb_col = sb("onesb_col", [128, 1], BF16)
    c.onesf_row = sb("onesf_row", [1, 128])
    c.oneone = sb("oneone", [1, 1])
    c.anw0_sb = sb("anw0_sb", [128, DT])
    c.eb0_sb = sb("eb0_sb", [128, E * DT])
    c.pb0_sb = sb("pb0_sb", [128, DT])
    c.fnw0_sb = sb("fnw0_sb", [128, DT])
    c.bqk1_sb = sb("bqk1_sb", [128, 2 * DT])
    c.bo1_sb = sb("bo1_sb", [128, DT])
    c.bv1_sb = sb("bv1_sb", [1, D])
    c.rw0_sb = sb("rw0_sb", [128, DT * E])
    c.gw1_sb = sb("gw1_sb", [128, DT * E])
    c.rb0_sb = sb("rb0_sb", [1, E])
    c.eb1r_sb = sb("eb1r_sb", [1, E])
    c.ebc1_sb = sb("ebc1_sb", [128, E])
    c.pm0_sb = sb("pm0_sb", [128, NH * NP], BF16)
    c.eps_ln = sb("eps_ln", [128, 1])
    c.eps_rms = sb("eps_rms", [128, 1])
    c.bqk1s_sb = sb("bqk1s_sb", [128, DT])
    c.pm1_sb = sb("pm1_sb", [16, NH * DH], BF16)

    p.dma("sp", c.identf_sb[:, :], c.identf)
    p.dma("sp", c.identb_sb[:, :], c.identb)
    p.dma("sp", c.swap_sb[:, :], c.swapT)
    p.dma("sp", c.anw0_sb[:, :], c.anw0c)
    p.dma("sp", c.eb0_sb[:, :], c.eb0c)
    p.dma("sp", c.pb0_sb[:, :], c.pb0c)
    p.dma("sp", c.fnw0_sb[:, :], c.fnw0c)
    p.dma("sp", c.bqk1_sb[:, :], c.bqk1c)
    p.dma("sp", c.bo1_sb[:, :], c.bo1c)
    p.dma("sp", c.bv1_sb[:, :], c.bv1)
    p.dma("sp", c.rw0_sb[:, :].rearrange("p (dt e) -> p dt e", e=E),
          c.rw0.rearrange("(dt p) e -> p dt e", p=128))
    p.dma("sp", c.gw1_sb[:, :].rearrange("p (dt e) -> p dt e", e=E),
          c.gw1.rearrange("(dt p) e -> p dt e", p=128))
    p.dma("sp", c.rb0_sb[:, :], c.rb0)
    p.dma("sp", c.eb1r_sb[:, :], c.ebias1)
    p.dma("pool", c.pm0_sb[:, :].rearrange("p (h n) -> p h n", n=NP),
          c.pm0T.rearrange("h p n -> p h n"))
    p.dma("pool", c.pm1_sb[:, :].rearrange("n (h d) -> n h d", d=DH),
          c.pm1.rearrange("h n d -> n h d"))
    for t, v in [(c.onesf_col, 1.0), (c.onesb_col, 1.0), (c.onesf_row, 1.0),
                 (c.oneone, 1.0), (c.eps_ln, LN_EPS), (c.eps_rms, RMS_EPS)]:
        p.op("pool", lambda e, t=t, v=v: e.memset(t[:, :], v), writes=[t])
    p.dma("sp", c.pmk_bf.rearrange("(h p) n -> p h n", p=128),
          c.pm0_sb[:, :].rearrange("p (h n) -> p h n", n=NP), tag="pmkout")
    p.ts("dve", c.bqk1s_sb[:, :], c.bqk1_sb[:, 0:DT], SCALE1, ALU.mult, tag="bqs")
    with nc.psum_tensor("ebc_ps", [128, 512], F32) as ebc_ps:
        p.mmgroup([(ebc_ps[:, 0:E], c.onesf_row[:, :], c.eb1r_sb[:, :], True, True)],
                  reads=[c.onesf_row, c.eb1r_sb], writes=[ebc_ps], tag="ebc")
        p.copy("dve", c.ebc1_sb[:, :], ebc_ps[:, 0:E])
    p.barrier()


def phase_preconvert(nc, p, c, ctx):
    bb = [ctx.enter_context(nc.sbuf_tensor(f"pcb{i}", [128, 6144], BF16))
          for i in range(2)]
    jobs = [(c.wqkv0T, c.wqkv0T_bf), (c.wout0T, c.wout0T_bf)] +            [(c.ew0[e], c.ew0_bf[e]) for e in range(E)] +            [(c.pw0, c.pw0_bf), (c.wqkv1T, c.wqkv1T_bf), (c.wout1T, c.wout1T_bf),
            (c.sw1, c.sw1_bf), (c.sw3, c.sw3_bf), (c.sw2, c.sw2_bf)] +            [(c.w11[e], c.w11_bf[e]) for e in range(E)] +            [(c.w31[e], c.w31_bf[e]) for e in range(E)] +            [(c.w21[e], c.w21_bf[e]) for e in range(E)]
    k = 0
    dims = "abcde"
    for srcap, dstap in jobs:
        L = 1
        for s in srcap.shape:
            L *= s
        spec = " ".join(dims[:len(srcap.shape)])
        sflat = srcap.rearrange(f"{spec} -> ({spec})").rearrange("(q y) -> q y", q=128)
        dflat = dstap.rearrange(f"{spec} -> ({spec})").rearrange("(q y) -> q y", q=128)
        Y = L // 128
        for y0 in range(0, Y, 6144):
            y1 = min(y0 + 6144, Y)
            buf = bb[k % 2]
            k += 1
            p.dma("pool", buf[:, 0:y1 - y0], sflat[:, y0:y1], tag="pcin")
            p.dma("pool", dflat[:, y0:y1], buf[:, 0:y1 - y0], tag="pcout")


def phase_ln_router(nc, p, c, layer, skip_prefix=None):
    T = c.T
    C = 512
    NSUB = C // 128
    x_src = c.xT if layer == 0 else c.x1T
    xn_dst = c.xn0T if layer == 0 else c.xn1T
    wr_dst = c.w0rT if layer == 0 else c.s1rT
    rw_sb = c.rw0_sb if layer == 0 else c.gw1_sb

    with ExitStack() as ctx:
        sb = lambda n, s, dt=F32: ctx.enter_context(nc.sbuf_tensor(f"L{layer}{n}", s, dt))
        ps = lambda n, s, dt=F32: ctx.enter_context(nc.psum_tensor(f"L{layer}{n}", s, dt))
        x32 = [sb(f"lr_x32_{i}", [128, DT * C]) for i in range(2)]
        xnf = sb("lr_xnf", [128, DT * C])
        xnb = sb("lr_xnb", [128, DT * C], BF16)
        xab = sb("lr_xab", [128, DT * C], BF16) if layer == 0 else None
        xsqb = sb("lr_xsqb", [128, DT * C], BF16)
        rows = sb("lr_rows", [1, 7 * C])
        lg = sb("lr_lg", [128, 8 * NSUB])
        m1 = sb("lr_m1", [128, NSUB])
        m2 = sb("lr_m2", [128, NSUB])
        eq = sb("lr_eq", [128, 8])
        msk = sb("lr_msk", [128, 8])
        ind = sb("lr_ind", [128, 8])
        sb_sel = sb("lr_sbsel", [128, 8])
        nm1 = sb("lr_nm1", [128, NSUB])
        ex = sb("lr_ex", [128, 8])
        e2 = sb("lr_e2", [128, 1])
        den = sb("lr_den", [128, 1])
        rec = sb("lr_rec", [128, NSUB])
        wts = sb("lr_wts", [128, 8])
        wrow = sb("lr_wrow", [8, 128])
        sum_ps = ps("lr_sum_ps", [1, 512])
        sum2_ps = ps("lr_sum2_ps", [1, 512])
        bc_ps = ps("lr_bc_ps", [128, C])
        bc2_ps = ps("lr_bc2_ps", [128, C])
        lg_ps = ps("lr_lg_ps", [128, 512])
        tr_ps = ps("lr_tr_ps", [128, 512])
        r2_ps = ps("lr_r2_ps", [128, C]) if layer == 0 else None

        def row(i, n=C):
            return rows[0:1, i * C:i * C + n]

        for ch in range(T // C):
            xb = x32[ch % 2]
            col = slice(ch * C, (ch + 1) * C)
            for h in range(2):
                sl = slice(h * 8, (h + 1) * 8)
                p.dma("sp", xb[:, :].rearrange("p (dt c) -> p dt c", c=C)[:, sl],
                      x_src.rearrange("(dt p) t -> p dt t", p=128)[:, sl, col],
                      tag=f"lrx{ch}")
            for dt in range(DT):
                xt = xb[:, dt * C:(dt + 1) * C]
                p.act(xsqb[:, dt * C:(dt + 1) * C], xt, AF.Square, tag="sq")
            p.mmgroup([(sum_ps[0:1, 0:C], c.onesf_col[:, :], xb[:, dt * C:(dt + 1) * C],
                        dt == 0, dt == DT - 1) for dt in range(DT)],
                      reads=[c.onesf_col, xb], writes=[sum_ps], tag="redx")
            p.mmgroup([(sum2_ps[0:1, 0:C], c.onesb_col[:, :], xsqb[:, dt * C:(dt + 1) * C],
                        dt == 0, dt == DT - 1) for dt in range(DT)],
                      reads=[c.onesb_col, xsqb], writes=[sum2_ps], tag="redx2")
            p.ts("dve", row(0), sum_ps[0:1, 0:C], 1.0 / D, ALU.mult, tag="m")
            p.ts("dve", row(1), sum2_ps[0:1, 0:C], 1.0 / D, ALU.mult, tag="q")
            p.tt("dve", row(2), row(0), row(0), ALU.mult, tag="mm")
            p.tt("dve", row(2), row(1), row(2), ALU.subtract, tag="var")
            p.act(row(3), row(2), AF.Ln, bias=c.eps_ln[0:1, :], tag="lnv")
            p.act(row(3), row(3), AF.Exp, scale=-0.5, tag="rstd")
            p.tt("dve", row(4), row(0), row(3), ALU.mult, tag="mr")
            p.mmgroup([(bc_ps[:, :], c.onesf_row[:, :], row(3), True, True)],
                      reads=[c.onesf_row, rows], writes=[bc_ps], tag="bcR")
            p.mmgroup([(bc2_ps[:, :], c.onesf_row[:, :], row(4), True, True)],
                      reads=[c.onesf_row, rows], writes=[bc2_ps], tag="bcMR")
            for dt in range(DT):
                xt = xb[:, dt * C:(dt + 1) * C]
                xnfs = xnf[:, dt * C:(dt + 1) * C]
                p.tt("dve", xnfs, xt, bc_ps[:, :], ALU.mult, tag="xr")
                p.tt("dve", xnfs, xnfs, bc2_ps[:, :], ALU.subtract, tag="xn")
                p.copy("act", xnb[:, dt * C:(dt + 1) * C],
                       xnf[:, dt * C:(dt + 1) * C], tag="xnb")
            for s in range(NSUB):
                lgp = lg_ps[:, s * 8:s * 8 + 8]
                mms = [(lgp, c.onesf_row[:, :], c.rb0_sb[:, :], True, False)] if layer == 0 \
                    else []
                st0 = layer != 0
                for dt in range(DT):
                    mms.append((lgp, xnf[:, dt * C + s * 128: dt * C + s * 128 + 128],
                                rw_sb[:, :].rearrange("p (dt e) -> p dt e", e=E)[:, dt],
                                st0 and dt == 0, dt == DT - 1))
                p.mmgroup(mms, reads=[xnf, rw_sb] + ([c.rb0_sb] if layer == 0 else []),
                          writes=[lg_ps], tag=f"lg{s}")
                lgs = lg[:, s * 8:s * 8 + 8]
                m1s = m1[:, s:s + 1]
                if layer == 0:
                    p.copy("dve", lgs, lgp, tag="lgcp")
                else:
                    p.red(m1s, lgp, ALU.max, tag="gm")
                    p.ts("dve", nm1[:, s:s + 1], m1s, -1.0, ALU.mult, tag="gnm")
                    p.act(ex[:, :], lgp, AF.Exp, bias=nm1[:, s:s + 1],
                          accum=den[:, :], tag="gex")
                    p.recip(rec[:, s:s + 1], den[:, :], tag="grec")
                    p.ts("dve", lgs, ex[:, :], rec[:, s:s + 1], ALU.mult, tag="gsc")
                if layer == 1:
                    p.tt("dve", sb_sel[:, :], lgs, c.ebc1_sb[:, :], ALU.add, tag="gsb")
                    selap = sb_sel[:, :]
                else:
                    selap = lgs
                p.red(m1s, selap, ALU.max, tag="m1")
                p.ts("dve", eq[:, :], selap, m1s, ALU.is_ge, tag="eq")
                p.stt("dve", msk[:, :], eq[:, :], NEG, selap, ALU.mult, ALU.add, tag="msk")
                p.red(m2[:, s:s + 1], msk[:, :], ALU.max, tag="m2")
                p.ts("dve", ind[:, :], selap, m2[:, s:s + 1], ALU.is_ge, tag="ind")
                if layer == 0:
                    p.ts("dve", nm1[:, s:s + 1], m1s, -1.0, ALU.mult, tag="nm1")
                    p.act(ex[:, :], lgs, AF.Exp, bias=nm1[:, s:s + 1], tag="exl")
                    p.act(e2[:, :], m2[:, s:s + 1], AF.Exp, bias=nm1[:, s:s + 1], tag="ex2")
                    p.ts("dve", den[:, :], e2[:, :], 1.0, ALU.add, tag="den")
                    p.recip(rec[:, s:s + 1], den[:, :], tag="rec")
                    p.stt("dve", wts[:, :], ex[:, :], rec[:, s:s + 1], ind[:, :],
                          ALU.mult, ALU.mult, tag="wts")
                else:
                    p.tt("dve", wts[:, :], lgs, ind[:, :], ALU.mult, tag="wts")
                p.transp(tr_ps[0:8, 0:128], wts[:, :], c.identf_sb[:, :], tag="wtr")
                p.copy("dve", wrow[:, :], tr_ps[0:8, 0:128], tag="wrowcp")
                p.dma("sp" if layer == 0 else "pool",
                      wr_dst[:, ch * C + s * 128: ch * C + (s + 1) * 128],
                      wrow[:, :], tag="wrow")
            if layer == 0:
                for dt in range(DT):
                    p.act(xsqb[:, dt * C:(dt + 1) * C], xnb[:, dt * C:(dt + 1) * C],
                          AF.Square, tag="sq2")
                p.mmgroup([(sum2_ps[0:1, 0:C], c.onesb_col[:, :],
                            xsqb[:, dt * C:(dt + 1) * C], dt == 0, dt == DT - 1)
                           for dt in range(DT)],
                          reads=[c.onesb_col, xsqb], writes=[sum2_ps], tag="redn2")
                p.ts("dve", row(5), sum2_ps[0:1, 0:C], 1.0 / D, ALU.mult, tag="q2")
                p.act(row(6), row(5), AF.Ln, bias=c.eps_rms[0:1, :], tag="lnv2")
                p.act(row(6), row(6), AF.Exp, scale=-0.5, tag="rstd2")
                p.mmgroup([(r2_ps[:, :], c.onesf_row[:, :], row(6), True, True)],
                          reads=[c.onesf_row, rows], writes=[r2_ps], tag="bcR2")
                for dt in range(DT):
                    p.stt("dve", xab[:, dt * C:(dt + 1) * C], xnb[:, dt * C:(dt + 1) * C],
                          c.anw0_sb[:, dt:dt + 1], r2_ps[:, :], ALU.mult, ALU.mult,
                          tag="xa")
            for h in range(2):
                sl = slice(h * 8, (h + 1) * 8)
                p.dma("sp" if layer == 0 else "pool",
                      xn_dst.rearrange("(dt p) t -> p dt t", p=128)[:, sl, col],
                      xnb[:, :].rearrange("p (dt c) -> p dt c", c=C)[:, sl], tag="xnsp")
                if layer == 0:
                    p.dma("sp", c.xa0T.rearrange("(dt p) t -> p dt t", p=128)[:, sl, col],
                          xab[:, :].rearrange("p (dt c) -> p dt c", c=C)[:, sl], tag="xasp")
    p.barrier(skip_prefix=skip_prefix)


def build_program(T=4096, phases=("ln0", "qkv0", "attn0", "moe0",
                                  "ln1", "qkv1", "attn1", "wout1", "moe1")):
    nc = bass.Bass("TRN2", target_bir_lowering=False)
    c = declare_io(nc, T)
    p = Prog(nc)
    with ExitStack() as ctx:
        p.init_sems(ctx)
        load_persistent(nc, p, c, ctx)
        with ExitStack() as pcctx:
            phase_preconvert(nc, p, c, pcctx)
            if "ln0" in phases:
                p.mark("ln0")
                phase_ln_router(nc, p, c, 0, skip_prefix="poolq")
            if "qkv0" in phases:
                p.mark("qkv0")
                phase_qkv(nc, p, c, 0, skip_prefix="poolq")
            p.barrier()
        if "attn0" in phases:
            p.mark("attn0")
            phase_attn0(nc, p, c)
        if "moe0" in phases:
            p.mark("moe0")
            phase_moe0(nc, p, c)
        if "ln1" in phases:
            p.mark("ln1")
            phase_ln_router(nc, p, c, 1)
        if "qkv1" in phases:
            p.mark("qkv1")
            phase_qkv(nc, p, c, 1)
        if "attn1" in phases:
            p.mark("attn1")
            phase_attn1(nc, p, c)
        if "wout1" in phases:
            p.mark("wout1")
            phase_wout1(nc, p, c)
        if "moe1" in phases:
            p.mark("moe1")
            phase_moe1(nc, p, c)
        p.mark("end")
        p.barrier()
        p.emit()
    nc._marks = p.marks
    return nc



def _bf(x):
    return np.asarray(x, dtype=ml_dtypes.bfloat16)


def host_constants(T):
    bf = ml_dtypes.bfloat16
    ident = np.eye(128, dtype=np.float32)
    S = np.zeros((128, 128), dtype=np.float32)
    for i in range(64):
        S[2 * i + 1, 2 * i] = -1.0
        S[2 * i, 2 * i + 1] = 1.0
    inv_freq = 1.0 / (10000.0 ** (np.arange(0, DH, 2, dtype=np.float32) / DH))
    pos = np.arange(T, dtype=np.float32)
    fr = pos[None, :] * inv_freq[:, None]
    fr2 = np.repeat(fr, 2, axis=0)
    cosT = np.cos(fr2)
    sinT = np.sin(fr2)
    q_len, k_len, pq, pk = WIN, WIN + NP, NP, 2 * NP
    bias = np.full((q_len, k_len), NEG, dtype=np.float32)
    bias[:, :pk] = 0.0
    sq = q_len - pq
    causal = np.tril(np.ones((sq, sq), dtype=bool))
    bias[pq:, pk:] = np.where(causal, 0.0, NEG)
    maskW = np.concatenate([np.zeros((q_len, NP), np.float32), bias[:, NP:]],
                           axis=1).reshape(4, 128, WIN + NP)
    return {
        "identf": ident, "identb": _bf(ident), "swapT": _bf(S),
        "cosT": _bf(cosT), "sinT": _bf(sinT), "maskW": _bf(maskW * SCALE0),
    }


def host_inputs(inp, T):
    f32 = lambda a: np.ascontiguousarray(np.asarray(a, dtype=np.float32))
    col = lambda a: f32(np.asarray(a).reshape(-1, 128).T)
    def qkvL(w):
        return f32(np.asarray(w).T.reshape(DT, 128, 6, 1024).transpose(2, 1, 0, 3))

    def outL(w):
        return f32(np.asarray(w).T.reshape(DT, 128, D).transpose(1, 0, 2))

    def ewL(w):
        F = np.asarray(w).shape[1]
        return f32(np.asarray(w).reshape(DT, 128, F // 512, 512).transpose(2, 1, 0, 3))

    def hbL(w):
        return f32(np.asarray(w).reshape(DT, 128, 7, 256).transpose(2, 1, 0, 3))

    def w2L(w):
        return f32(np.asarray(w).reshape(2, 7, 128, D).transpose(0, 2, 1, 3))

    d = {
        "wqkv0T": qkvL(inp["L0_attn_w_qkv"]),
        "wout0T": outL(inp["L0_attn_w_out"]),
        "pm0T": f32(np.asarray(inp["L0_attn_pm"])[0].transpose(0, 2, 1)),
        "pm1": f32(np.asarray(inp["L0_attn_pm"])[1]),
        "anw0c": col(inp["L0_attn_norm_w"]),
        "rw0": f32(inp["L0_ffn_router_w"]),
        "rb0": f32(np.asarray(inp["L0_ffn_router_b"]).reshape(1, E)),
        "ew0": np.stack([ewL(w) for w in np.asarray(inp["L0_ffn_expert_w"], dtype=np.float32)]),
        "eb0c": f32(np.asarray(inp["L0_ffn_expert_b"]).reshape(E * DT, 128).T),
        "pw0": ewL(inp["L0_ffn_proj_w"]),
        "pb0c": col(inp["L0_ffn_proj_b"]),
        "fnw0c": col(inp["L0_ffn_norm_w"]),
        "wqkv1T": qkvL(inp["L1_attn_w_qkv"]),
        "bqk1c": col(np.asarray(inp["L1_attn_b_qkv"])[:2 * D]),
        "bv1": f32(np.asarray(inp["L1_attn_b_qkv"])[2 * D:].reshape(1, D)),
        "wout1T": outL(inp["L1_attn_w_out"]),
        "bo1c": col(inp["L1_attn_b_out"]),
        "gw1": f32(inp["L1_ffn_gate_w"]),
        "ebias1": f32(np.asarray(inp["L1_ffn_expert_bias"]).reshape(1, E)),
        "w11": np.stack([hbL(w) for w in np.asarray(inp["L1_ffn_w1"], dtype=np.float32)]),
        "w21": np.stack([w2L(w) for w in np.asarray(inp["L1_ffn_w2"], dtype=np.float32)]),
        "w31": np.stack([hbL(w) for w in np.asarray(inp["L1_ffn_w3"], dtype=np.float32)]),
        "sw1": hbL(inp["L1_ffn_sw1"]),
        "sw2": w2L(inp["L1_ffn_sw2"]),
        "sw3": hbL(inp["L1_ffn_sw3"]),
    }
    d.update(host_constants(T))
    return d


def kernel(**inputs):
    T = 4096
    x = np.asarray(inputs["L0_attn_x"], dtype=np.float32)
    shared = host_inputs(inputs, T)
    nc = build_program(T)
    in_maps = []
    for i in range(8):
        m = dict(shared)
        m["xT"] = np.ascontiguousarray(x[i].T)
        in_maps.append(m)
    res = run_bass_kernel_spmd(nc, in_maps, core_ids=list(range(8)))
    out = np.stack([np.asarray(r["outT"]).T for r in res.results], axis=0)
    return np.ascontiguousarray(out)


def phase_qkv(nc, p, c, layer, skip_prefix=None):
    T = c.T
    TC = min(1024, T)
    NT = T // TC
    wsrc = c.wqkv0T_bf if layer == 0 else c.wqkv1T_bf
    asrc = c.xa0T if layer == 0 else c.xn1T
    qdst = c.q0T if layer == 0 else c.q1T
    kdst = c.k0T if layer == 0 else c.k1T
    vdst = c.v0 if layer == 0 else c.v1

    with ExitStack() as ctx:
        sb = lambda n, s, dt=F32: ctx.enter_context(nc.sbuf_tensor(f"qk{layer}{n}", s, dt))
        ps = lambda n, s: ctx.enter_context(nc.psum_tensor(f"qk{layer}{n}", s, F32))
        wb = [sb(f"w{i}", [128, DT * 1024], BF16) for i in range(2)]
        ab = [sb(f"a{i}", [128, DT * TC], BF16) for i in range(2)]
        qr = [sb(f"qr{i}", [128, TC], BF16) for i in range(2)]
        t1 = [sb(f"t1{i}", [128, TC], BF16) for i in range(2)]
        t2 = [sb(f"t2{i}", [128, TC], BF16) for i in range(2)]
        qro = [sb(f"qro{i}", [128, TC], BF16) for i in range(2)]
        vo = [sb(f"vo{i}", [128, 1024], BF16) for i in range(2)]
        if layer == 0:
            cos_sb = sb("cos", [128, T], BF16)
            sin_sb = sb("sin", [128, T], BF16)
            p.dma("sp", cos_sb[:, :], c.cosT)
            p.dma("sp", sin_sb[:, :], c.sinT)
        pp = [ps(f"pp{i}", [128, 1024]) for i in range(2)]
        rp = [ps(f"rp{i}", [128, TC]) for i in range(2)]

        def wtile(b, dt, o0, n):
            return wb[b][:, dt * 1024 + o0: dt * 1024 + o0 + n]

        def atile(b, dt, t0, n):
            return ab[b][:, dt * TC + t0: dt * TC + t0 + n]

        for ob in range(6):
            wcur = wb[ob % 2]
            p.dma("sp", wcur[:, :].rearrange("p (dt o) -> p dt o", o=1024),
                  wsrc[ob], tag=f"qw{ob}")
            for tch in range(NT):
                acur = ab[tch % 2] if NT > 1 else ab[0]
                if ob == 0 or NT > 1:
                    for h in range(2):
                        sl = slice(h * 8, (h + 1) * 8)
                        p.dma("sp", acur[:, :].rearrange("p (dt t) -> p dt t", t=TC)[:, sl],
                              asrc.rearrange("(dt p) t -> p dt t", p=128)
                              [:, sl, tch * TC:(tch + 1) * TC], tag=f"qa{tch}")
                is_v = ob >= 4
                if not is_v:
                    for ot in range(8):
                        o_glob = ob * 1024 + ot * 128
                        qp = pp[ot % 2]
                        mms = []
                        for dt in range(DT):
                            for half in range(TC // 512):
                                mms.append((qp[:, half * 512:(half + 1) * 512],
                                            wtile(ob % 2, dt, ot * 128, 128),
                                            atile(tch % 2 if NT > 1 else 0, dt,
                                                  half * 512, 512),
                                            dt == 0, dt == DT - 1))
                        p.mmgroup(mms, reads=[wcur, acur], writes=[qp], tag=f"qkmm{ob}")
                        pr = ot % 2
                        if layer == 0:
                            qsc = SCALE0 if o_glob < D else 1.0
                            p.op("act", lambda e, qp=qp, pr=pr, qsc=qsc:
                                 e.activation(qr[pr][:, :], qp[:, 0:TC], AF.Copy,
                                              scale=qsc),
                                 reads=[qp], writes=[qr[pr]], tag="qevac")
                            rmm = [(rp[pr][:, half * 512:(half + 1) * 512],
                                    c.swap_sb[:, :],
                                    qr[pr][:, half * 512:(half + 1) * 512], True, True)
                                   for half in range(TC // 512)]
                            p.mmgroup(rmm, reads=[c.swap_sb, qr[pr]], writes=[rp[pr]],
                                      tag="rot")
                            cs = cos_sb[:, tch * TC:(tch + 1) * TC]
                            sn = sin_sb[:, tch * TC:(tch + 1) * TC]
                            p.tt("dve", t1[pr][:, :], qr[pr][:, :], cs, ALU.mult, tag="rc")
                            p.tt("dve", t2[pr][:, :], rp[pr][:, 0:TC], sn, ALU.mult,
                                 tag="rs")
                            p.tt("dve", qro[pr][:, :], t1[pr][:, :], t2[pr][:, :],
                                 ALU.add, tag="radd")
                        else:
                            if o_glob < D:
                                bcol = c.bqk1s_sb[:, ob * 8 + ot: ob * 8 + ot + 1]
                                sc1 = SCALE1
                            else:
                                bcol = c.bqk1_sb[:, ob * 8 + ot: ob * 8 + ot + 1]
                                sc1 = 1.0
                            p.ts("dve", qro[pr][:, :], qp[:, 0:TC], sc1,
                                 ALU.mult, bcol, ALU.add, tag="qbias")
                        dst = qdst if o_glob < D else kdst
                        r0 = o_glob if o_glob < D else o_glob - D
                        p.dma("sp", dst[r0:r0 + 128, tch * TC:(tch + 1) * TC],
                              qro[pr][:, :], tag="qout")
                else:
                    vb0 = (ob - 4) * 1024
                    for tt in range(TC // 128):
                        vp = pp[tt % 2]
                        mms = []
                        if layer == 1:
                            for half in range(2):
                                mms.append((vp[:, half * 512:(half + 1) * 512],
                                            c.onesf_row[:, :],
                                            c.bv1_sb[0:1, vb0 + half * 512: vb0 + (half + 1) * 512],
                                            True, False))
                        st0 = layer == 0
                        for dt in range(DT):
                            for half in range(2):
                                mms.append((vp[:, half * 512:(half + 1) * 512],
                                            atile(tch % 2 if NT > 1 else 0, dt,
                                                  tt * 128, 128),
                                            wtile(ob % 2, dt, half * 512, 512),
                                            st0 and dt == 0, dt == DT - 1))
                        rds = [wcur, acur] + ([c.onesf_row, c.bv1_sb] if layer == 1 else [])
                        p.mmgroup(mms, reads=rds, writes=[vp], tag=f"vmm{ob}")
                        p.copy("act", vo[tt % 2][:, :], vp[:, :], tag="vevac")
                        p.dma("sp", vdst[tch * TC + tt * 128: tch * TC + (tt + 1) * 128,
                                         vb0:vb0 + 1024], vo[tt % 2][:, :], tag="vout")
    p.barrier(skip_prefix=skip_prefix)


def phase_attn0(nc, p, c):
    T = c.T
    NW = T // WIN
    KL = NP + WIN

    with ExitStack() as ctx:
        sb = lambda n, s, dt=F32: ctx.enter_context(nc.sbuf_tensor("a0" + n, s, dt))
        ps = lambda n, s, dt=F32: ctx.enter_context(nc.psum_tensor("a0" + n, s, dt))
        wout_sb = sb("wout", [128, DT * D], BF16)
        mask_sb = sb("mask", [128, 4 * KL], BF16)
        qw = [sb(f"q{i}", [128, NH * WIN], BF16) for i in range(2)]
        kw = [sb(f"k{i}", [128, NH * KL], BF16) for i in range(2)]
        vw = [sb(f"v{i}", [128, 4 * D], BF16) for i in range(2)]
        ao = sb("ao", [128, NH * WIN], BF16)
        pws = [sb(f"pws{i}", [128, KL], BF16) for i in range(2)]
        pwn = [sb(f"pwn{i}", [128, KL], BF16) for i in range(2)]
        ptw = sb("ptw", [128, 4 * WIN], BF16)
        ptp = sb("ptp", [16, WIN], BF16)
        wo = sb("wo", [128, WIN], BF16)
        r1 = [sb(f"r1{i}", [128, 1]) for i in range(2)]
        nm = [sb(f"nm{i}", [128, 1]) for i in range(2)]
        aW = [sb(f"aW{i}", [128, 1]) for i in range(2)]
        rec = [sb(f"rec{i}", [128, 1]) for i in range(2)]
        simp = [ps(f"sim{i}", [128, 1024]) for i in range(2)]
        trp = ps("trp", [128, 1024], BF16)
        ovp = ps("ovp", [128, 512])
        wop = [ps(f"wop{i}", [128, 512]) for i in range(2)]

        p.dma("sp", wout_sb[:, :].rearrange("p (dt o) -> p dt o", o=D),
              c.wout0T_bf, tag="woutld")
        p.dma("sp", mask_sb[:, :].rearrange("p (qs k) -> p qs k", k=KL),
              c.maskW.rearrange("qs p k -> p qs k"), tag="maskld")

        for w in range(NW):
            col = slice(w * WIN, (w + 1) * WIN)
            qc, kc, vc = qw[w % 2], kw[w % 2], vw[w % 2]
            p.dma("sp", qc[:, :].rearrange("p (h t) -> p h t", t=WIN),
                  c.q0T.rearrange("(h p) t -> p h t", p=128)[:, :, col], tag="qld")
            p.dma("sp", kc[:, :].rearrange("p (h t) -> p h t", t=KL)[:, :, 0:NP],
                  c.pmk_bf.rearrange("(h p) n -> p h n", p=128), tag="kpld")
            p.dma("sp", kc[:, :].rearrange("p (h t) -> p h t", t=KL)[:, :, NP:KL],
                  c.k0T.rearrange("(h p) t -> p h t", p=128)[:, :, col], tag="kld")
            p.dma("sp", vc[:, :].rearrange("p (kt d) -> p kt d", d=D),
                  c.v0.rearrange("(kt p) d -> p kt d", p=128)[:, w * 4:(w + 1) * 4],
                  tag="vld")
            for h in range(NH):
                for qs in range(4):
                    pr = qs % 2
                    sim = simp[pr]
                    qt = qc[:, h * WIN + qs * 128: h * WIN + (qs + 1) * 128]
                    p.mmgroup(
                        [(sim[:, 0:512], c.identb_sb[:, :],
                          mask_sb[:, qs * KL: qs * KL + 512], True, False),
                         (sim[:, 512:KL], c.identb_sb[:, :],
                          mask_sb[:, qs * KL + 512: (qs + 1) * KL], True, False),
                         (sim[:, 0:512], qt, kc[:, h * KL: h * KL + 512], False, True),
                         (sim[:, 512:KL], qt, kc[:, h * KL + 512: h * KL + KL],
                          False, True)],
                        reads=[qc, kc, c.identb_sb, mask_sb], writes=[sim], tag="simW")
                    p.red(nm[pr][:, :], sim[:, 0:KL], ALU.max, negate=True, tag="r1")
                    p.act(pws[pr][:, :], sim[:, 0:KL], AF.Exp, bias=nm[pr][:, :],
                          accum=aW[pr][:, :], tag="expW")
                    p.recip(rec[pr][:, :], aW[pr][:, :], tag="rec")
                    p.ts("dve", pwn[pr][:, :], pws[pr][:, :], rec[pr][:, :], ALU.mult,
                         tag="pw")
                    for kt in range(4):
                        p.transp(trp[:, kt * 128:(kt + 1) * 128],
                                 pwn[pr][:, NP + kt * 128: NP + (kt + 1) * 128],
                                 c.identb_sb[:, :], tag="ptr")
                    p.transp(trp[0:NP, 512:640], pwn[pr][:, 0:NP],
                             c.identb_sb[:, :], tag="ptrP")
                    p.copy("dve",
                           ptw[:, :].rearrange("p (kt q) -> p kt q", q=WIN)
                           [:, :, qs * 128:(qs + 1) * 128],
                           trp[:, 0:512].rearrange("p (kt q) -> p kt q", q=128),
                           tag="ptev")
                    p.copy("dve", ptp[0:NP, qs * 128:(qs + 1) * 128],
                           trp[0:NP, 512:640], tag="ptevP")
                mms = [(ovp[:, :], c.pm1_sb[0:NP, h * DH:(h + 1) * DH],
                        ptp[0:NP, :], True, False)]
                for kt in range(4):
                    mms.append((ovp[:, :], vc[:, kt * D + h * DH: kt * D + (h + 1) * DH],
                                ptw[:, kt * WIN:(kt + 1) * WIN], False, kt == 3))
                p.mmgroup(mms, reads=[c.pm1_sb, ptp, vc, ptw], writes=[ovp], tag="pv")
                p.copy("act", ao[:, h * WIN:(h + 1) * WIN], ovp[:, :], tag="aoev")
            for ot in range(DT):
                wp = wop[ot % 2]
                p.mmgroup([(wp[:, :], wout_sb[:, dt * D + ot * 128: dt * D + (ot + 1) * 128],
                            ao[:, dt * WIN:(dt + 1) * WIN], dt == 0, dt == DT - 1)
                           for dt in range(DT)],
                          reads=[wout_sb, ao], writes=[wp], tag="womm")
                p.copy("act", wo[:, :], wp[:, :], tag="woev")
                p.dma("pool", c.at0T[ot * 128:(ot + 1) * 128, col], wo[:, :], tag="woout")
    p.barrier()


def phase_moe0(nc, p, c):
    T = c.T
    C = min(1024, T)

    with ExitStack() as ctx:
        sb = lambda n, s, dt=F32: ctx.enter_context(nc.sbuf_tensor("m0" + n, s, dt))
        ps = lambda n, s, dt=F32: ctx.enter_context(nc.psum_tensor("m0" + n, s, dt))
        xnbs = [sb(f"xn{i}", [128, DT * C], BF16) for i in range(2)]
        comb = sb("comb", [128, DT * C], BF16)
        ew = [sb(f"ew{i}", [128, DT * 512], BF16) for i in range(2)]
        silu = sb("silu", [128, C], BF16)
        tmp = sb("tmp", [128, C], BF16)
        ffq = sb("ffq", [128, C], BF16)
        wrow = sb("wrow", [1, C])
        wbc = sb("wbc", [128, C], BF16)
        yb = sb("y", [128, DT * C], BF16)
        rows = sb("rows", [1, C])
        x32t = sb("x32t", [128, 2 * C])
        at0t = sb("at0t", [128, 2 * C], BF16)
        hp = [ps(f"hp{i}", [128, 1024]) for i in range(2)]
        bcps = ps("bcps", [128, C])
        s2ps = ps("s2ps", [128, 1024])

        NCH = T // C
        for ch in range(NCH):
            col = slice(ch * C, (ch + 1) * C)
            xnb = xnbs[ch % 2]
            for h in range(2):
                sl = slice(h * 8, (h + 1) * 8)
                p.dma("sp", xnb[:, :].rearrange("p (dt t) -> p dt t", t=C)[:, sl],
                      c.xn0T.rearrange("(dt p) t -> p dt t", p=128)[:, sl, col],
                      tag="xnld")
            for e in range(E):
                p.dma("sp", wrow[:, :], c.w0rT[e:e + 1, col], tag="wrow")
                p.mmgroup([(bcps[:, 0:512], c.onesf_row[:, :], wrow[0:1, 0:512],
                            True, True)] +
                          ([(bcps[:, 512:1024], c.onesf_row[:, :], wrow[0:1, 512:1024],
                             True, True)] if C > 512 else []),
                          reads=[c.onesf_row, wrow], writes=[bcps], tag="wbc")
                p.copy("dve", wbc[:, :], bcps[:, 0:C], tag="wbccp")
                for fh in range(4):
                    ewc = ew[fh % 2]
                    p.dma("sp", ewc[:, :].rearrange("p (dt f) -> p dt f", f=512),
                          c.ew0_bf[e, fh], tag="ewld")
                    for ft8 in range(4):
                        ftg = fh * 4 + ft8
                        hpp = hp[ft8 % 2]
                        mms = []
                        for dt in range(DT):
                            for half in range(C // 512):
                                mms.append((hpp[:, half * 512:(half + 1) * 512],
                                            ewc[:, dt * 512 + ft8 * 128:
                                                dt * 512 + (ft8 + 1) * 128],
                                            xnb[:, dt * C + half * 512:
                                                dt * C + (half + 1) * 512],
                                            dt == 0, dt == DT - 1))
                        p.mmgroup(mms, reads=[ewc, xnb], writes=[hpp], tag="hmm")
                        p.act(silu[:, :], hpp[:, 0:C], AF.Silu,
                              bias=c.eb0_sb[:, e * DT + ftg: e * DT + ftg + 1],
                              tag="silu")
                        if e == 0:
                            p.tt("dve", comb[:, ftg * C:(ftg + 1) * C], silu[:, :],
                                 wbc[:, :], ALU.mult, tag="cmb0")
                        else:
                            p.tt("dve", tmp[:, :], silu[:, :], wbc[:, :], ALU.mult,
                                 tag="cmbm")
                            p.tt("dve", comb[:, ftg * C:(ftg + 1) * C],
                                 comb[:, ftg * C:(ftg + 1) * C], tmp[:, :], ALU.add,
                                 tag="cmba")
            for fh in range(4):
                ewc = ew[1]
                p.dma("sp", ewc[:, :].rearrange("p (dt f) -> p dt f", f=512),
                      c.pw0_bf[fh], tag="pwld")
                for ot8 in range(4):
                    otg = fh * 4 + ot8
                    hpp = hp[ot8 % 2]
                    mms = []
                    for dt in range(DT):
                        for half in range(C // 512):
                            mms.append((hpp[:, half * 512:(half + 1) * 512],
                                        ewc[:, dt * 512 + ot8 * 128:
                                            dt * 512 + (ot8 + 1) * 128],
                                        comb[:, dt * C + half * 512:
                                             dt * C + (half + 1) * 512],
                                        dt == 0, dt == DT - 1))
                    p.mmgroup(mms, reads=[ewc, comb], writes=[hpp], tag="pmm")
                    p.stt("dve", yb[:, otg * C:(otg + 1) * C], hpp[:, 0:C],
                          c.pb0_sb[:, otg:otg + 1], xnb[:, otg * C:(otg + 1) * C],
                          ALU.add, ALU.add, tag="y")
            for ot in range(DT):
                p.act(silu[:, :], yb[:, ot * C:(ot + 1) * C], AF.Square, tag="ysq")
                p.mmgroup([(s2ps[0:1, half * 512:(half + 1) * 512], c.onesb_col[:, :],
                            silu[:, half * 512:(half + 1) * 512],
                            ot == 0, ot == DT - 1) for half in range(C // 512)],
                          reads=[c.onesb_col, silu], writes=[s2ps], tag="ys")
            p.ts("dve", rows[0:1, 0:C], s2ps[0:1, 0:C], 1.0 / D, ALU.mult, tag="ym")
            p.act(rows[0:1, 0:C], rows[0:1, 0:C], AF.Ln,
                  bias=c.eps_rms[0:1, :], tag="yln")
            p.act(rows[0:1, 0:C], rows[0:1, 0:C], AF.Exp, scale=-0.5, tag="yex")
            p.mmgroup([(bcps[:, half * 512:(half + 1) * 512], c.onesf_row[:, :],
                        rows[0:1, half * 512:(half + 1) * 512], True, True)
                       for half in range(C // 512)],
                      reads=[c.onesf_row, rows], writes=[bcps], tag="ybc")
            for q2 in range(8):
                ots = range(q2 * 2, (q2 + 1) * 2)
                p.dma("sp", x32t[:, :].rearrange("p (o t) -> p o t", t=C),
                      c.xT.rearrange("(o p) t -> p o t", p=128)
                      [:, q2 * 2:(q2 + 1) * 2, col], tag="xld")
                p.dma("sp", at0t[:, :].rearrange("p (o t) -> p o t", t=C),
                      c.at0T.rearrange("(o p) t -> p o t", p=128)
                      [:, q2 * 2:(q2 + 1) * 2, col], tag="atld")
                p.tt("dve", x32t[:, :], x32t[:, :], at0t[:, :], ALU.add, tag="addat")
                for j, ot in enumerate(ots):
                    p.stt("dve", ffq[:, :], yb[:, ot * C:(ot + 1) * C],
                          c.fnw0_sb[:, ot:ot + 1], bcps[:, 0:C], ALU.mult, ALU.mult,
                          tag="ffn")
                    p.tt("dve", x32t[:, j * C:(j + 1) * C],
                         x32t[:, j * C:(j + 1) * C], ffq[:, :], ALU.add, tag="addx")
                p.dma("pool", c.x1T.rearrange("(o p) t -> p o t", p=128)
                      [:, q2 * 2:(q2 + 1) * 2, col],
                      x32t[:, :].rearrange("p (o t) -> p o t", t=C), tag="x1out")
    p.barrier()


def phase_attn1(nc, p, c):
    T = c.T
    QC = min(512, T)
    NQ = T // QC
    KB = min(1024, T)
    NKB = T // KB

    with ExitStack() as ctx:
        sb = lambda n, s, dt=F32: ctx.enter_context(nc.sbuf_tensor("a1" + n, s, dt))
        ps = lambda n, s, dt=F32: ctx.enter_context(nc.psum_tensor("a1" + n, s, dt))
        qc = sb("q", [128, DT * QC], BF16)
        kb = [sb(f"k{i}", [128, DT * KB], BF16) for i in range(2)]
        vb = [sb(f"v{i}", [128, (KB // 128) * D], BF16) for i in range(2)]
        pb = [sb(f"p{i}", [128, KB], BF16) for i in range(2)]
        ptb = [sb(f"pt{i}", [128, KB], BF16) for i in range(2)]
        ctxa = [sb(f"cx{i}", [128, D]) for i in range(QC // 128)]
        ctxb = sb("cb", [128, D], BF16)
        ctxt = sb("ct", [128, 128 * (DT // 2)], BF16)
        nmrun = sb("m", [128, QC // 128])
        arun = sb("a", [128, QC // 128])
        nblk = [sb(f"rb{i}", [128, 1]) for i in range(2)]
        ablk = [sb(f"ab{i}", [128, 1]) for i in range(2)]
        nmn = [sb(f"mn{i}", [128, 1]) for i in range(2)]
        corr = [sb(f"co{i}", [128, 1]) for i in range(2)]
        rec = sb("rc", [128, 1])
        simp = [ps(f"s{i}", [128, 1024]) for i in range(2)]
        pvp = [ps(f"pv{i}", [128, 512]) for i in range(2)]
        trp = ps("tr", [128, 1024], BF16)
        trc = ps("tc", [128, 512], BF16)

        for qch in range(NQ):
            qcol = slice(qch * QC, (qch + 1) * QC)
            for h in range(2):
                sl = slice(h * 8, (h + 1) * 8)
                p.dma("sp", qc[:, :].rearrange("p (dt t) -> p dt t", t=QC)[:, sl],
                      c.q1T.rearrange("(dt p) t -> p dt t", p=128)[:, sl, qcol],
                      tag="q1ld")
            for kbi in range(NKB):
                kcur, vcur = kb[kbi % 2], vb[kbi % 2]
                for h in range(2):
                    sl = slice(h * 8, (h + 1) * 8)
                    p.dma("sp", kcur[:, :].rearrange("p (dt t) -> p dt t", t=KB)[:, sl],
                          c.k1T.rearrange("(dt p) t -> p dt t", p=128)
                          [:, sl, kbi * KB:(kbi + 1) * KB], tag="k1ld")
                p.dma("sp", vcur[:, :].rearrange("p (kt d) -> p kt d", d=D),
                      c.v1.rearrange("(kt p) d -> p kt d", p=128)
                      [:, kbi * (KB // 128):(kbi + 1) * (KB // 128)], tag="v1ld")
                for qs in range(QC // 128):
                    pr = qs % 2
                    sp_ = simp[pr]
                    mms = []
                    for half in range(KB // 512):
                        for dt in range(DT):
                            mms.append((sp_[:, half * 512:(half + 1) * 512],
                                        qc[:, dt * QC + qs * 128: dt * QC + (qs + 1) * 128],
                                        kcur[:, dt * KB + half * 512:
                                             dt * KB + (half + 1) * 512],
                                        dt == 0, dt == DT - 1))
                    p.mmgroup(mms, reads=[qc, kcur], writes=[sp_], tag="sim1")
                    nmr = nmrun[:, qs:qs + 1]
                    ar = arun[:, qs:qs + 1]
                    p.red(nblk[pr][:, :], sp_[:, 0:KB], ALU.max, negate=True, tag="rb")
                    if kbi == 0:
                        p.copy("dve", nmr, nblk[pr][:, :], tag="m0")
                        p.act(pb[pr][:, :], sp_[:, 0:KB], AF.Exp, bias=nblk[pr][:, :],
                              accum=ablk[pr][:, :], tag="exp1")
                        p.copy("dve", ar, ablk[pr][:, :], tag="a0")
                    else:
                        p.tt("dve", nmn[pr][:, :], nmr, nblk[pr][:, :], ALU.min,
                             tag="mn")
                        p.act(corr[pr][:, :], nmr, AF.Exp, bias=nmn[pr][:, :],
                              scale=-1.0, tag="corr")
                        p.copy("dve", nmr, nmn[pr][:, :], tag="mupd")
                        p.act(pb[pr][:, :], sp_[:, 0:KB], AF.Exp, bias=nmn[pr][:, :],
                              accum=ablk[pr][:, :], tag="exp1")
                        p.stt("dve", ar, ar, corr[pr][:, :], ablk[pr][:, :], ALU.mult,
                              ALU.add, tag="aupd")
                    for kt in range(KB // 128):
                        p.transp(trp[:, kt * 128:(kt + 1) * 128],
                                 pb[pr][:, kt * 128:(kt + 1) * 128],
                                 c.identb_sb[:, :], tag="ptr1")
                    p.copy("dve", ptb[pr][:, :], trp[:, 0:KB], tag="ptev1")
                    for dc in range(4):
                        pv = pvp[dc % 2]
                        p.mmgroup([(pv[:, :], ptb[pr][:, kt * 128:(kt + 1) * 128],
                                    vcur[:, kt * D + dc * 512: kt * D + (dc + 1) * 512],
                                    kt == 0, kt == KB // 128 - 1)
                                   for kt in range(KB // 128)],
                                  reads=[ptb[pr], vcur], writes=[pv], tag="pv1")
                        cx = ctxa[qs][:, dc * 512:(dc + 1) * 512]
                        if kbi == 0:
                            p.copy("dve", cx, pv[:, :], tag="cx0")
                        else:
                            p.stt("dve", cx, cx, corr[pr][:, :], pv[:, :], ALU.mult,
                                  ALU.add, tag="cxu")
            for qs in range(QC // 128):
                p.recip(rec[:, :], arun[:, qs:qs + 1], tag="rec1")
                p.ts("dve", ctxb[:, :], ctxa[qs][:, :], rec[:, :], ALU.mult, tag="cnrm")
                for dh in range(2):
                    for j in range(DT // 2):
                        dt = dh * (DT // 2) + j
                        sl8 = j % 4
                        p.transp(trc[:, sl8 * 128:(sl8 + 1) * 128],
                                 ctxb[:, dt * 128:(dt + 1) * 128],
                                 c.identb_sb[:, :], tag="ctr")
                        p.copy("dve", ctxt[:, j * 128:(j + 1) * 128],
                               trc[:, sl8 * 128:(sl8 + 1) * 128], tag="ctev")
                    p.dma("pool", c.ctx1T.rearrange("(dt p) t -> p dt t", p=128)
                          [:, dh * (DT // 2):(dh + 1) * (DT // 2),
                           qch * QC + qs * 128: qch * QC + (qs + 1) * 128],
                          ctxt[:, :].rearrange("p (dt q) -> p dt q", q=128),
                          tag="ctxout")
    p.barrier()


def phase_wout1(nc, p, c):
    T = c.T
    TC = min(1024, T)
    with ExitStack() as ctx:
        sb = lambda n, s, dt=F32: ctx.enter_context(nc.sbuf_tensor("w1" + n, s, dt))
        ps = lambda n, s: ctx.enter_context(nc.psum_tensor("w1" + n, s, F32))
        wsb = sb("w", [128, DT * D], BF16)
        cb = [sb(f"c{i}", [128, DT * TC], BF16) for i in range(2)]
        ob = sb("o", [128, TC], BF16)
        wp = [ps(f"p{i}", [128, 512]) for i in range(4)]
        p.dma("sp", wsb[:, :].rearrange("p (dt o) -> p dt o", o=D),
              c.wout1T_bf, tag="w1ld")
        for tch in range(T // TC):
            col = slice(tch * TC, (tch + 1) * TC)
            cc = cb[tch % 2]
            for h in range(2):
                sl = slice(h * 8, (h + 1) * 8)
                p.dma("sp", cc[:, :].rearrange("p (dt t) -> p dt t", t=TC)[:, sl],
                      c.ctx1T.rearrange("(dt p) t -> p dt t", p=128)[:, sl, col],
                      tag="c1ld")
            for ot in range(DT):
                nh = TC // 512
                pp = [wp[(ot * nh + i) % 4] for i in range(nh)]
                mms = []
                for dt in range(DT):
                    for half in range(nh):
                        mms.append((pp[half][:, :],
                                    wsb[:, dt * D + ot * 128: dt * D + (ot + 1) * 128],
                                    cc[:, dt * TC + half * 512: dt * TC + (half + 1) * 512],
                                    dt == 0, dt == DT - 1))
                p.mmgroup(mms, reads=[wsb, cc], writes=pp, tag="w1mm")
                for half in range(nh):
                    p.op("act", lambda e, pp=pp, half=half, ot=ot:
                         e.add(ob[:, half * 512:(half + 1) * 512], pp[half][:, :],
                               c.bo1_sb[:, ot:ot + 1]),
                         reads=[pp[half], c.bo1_sb], writes=[ob], tag="w1b")
                p.dma("pool", c.at1T[ot * 128:(ot + 1) * 128, col], ob[:, 0:TC],
                      tag="w1out")
    p.barrier()


def phase_moe1(nc, p, c):
    T = c.T
    C = min(1024, T)
    NCH = T // C
    HB = 2

    with ExitStack() as ctx:
        sb = lambda n, s, dt=F32: ctx.enter_context(nc.sbuf_tensor("m1" + n, s, dt))
        ps = lambda n, s, dt=F32: ctx.enter_context(nc.psum_tensor("m1" + n, s, dt))
        xnb = sb("xn", [128, DT * C], BF16)
        outb = sb("out", [128, DT * C], BF16)
        hsb = sb("h", [128, HT * C], BF16)
        w13 = [sb(f"w13_{i}", [128, DT * HB * 128 * 2], BF16) for i in range(2)]
        w2sb = [sb(f"w2_{i}", [128, 7 * 1024], BF16) for i in range(2)]
        sbc = sb("sbc", [128, C], BF16)
        srow = sb("srow", [1, C])
        t1 = sb("t1", [128, C])
        t3 = sb("t3", [128, C], BF16)
        ab = sb("A", [128, C], BF16)
        x1t = sb("x1t", [128, 4 * C])
        at1t = sb("at1t", [128, 4 * C], BF16)
        u1p = ps("u1", [128, 1024])
        u3p = ps("u3", [128, 1024])
        op2 = [ps(f"o{i}", [128, 1024]) for i in range(2)]

        for ch in range(NCH):
            col = slice(ch * C, (ch + 1) * C)
            for h in range(2):
                sl = slice(h * 8, (h + 1) * 8)
                p.dma("sp", xnb[:, :].rearrange("p (dt t) -> p dt t", t=C)[:, sl],
                      c.xn1T.rearrange("(dt p) t -> p dt t", p=128)[:, sl, col],
                      tag="xn1ld")
            for ei in range(E + 1):
                routed = ei > 0
                e = ei - 1
                w2src = c.sw2_bf if not routed else c.w21_bf[e]
                if routed:
                    p.dma("sp", srow[:, :], c.s1rT[e:e + 1, col], tag="srow")
                    p.mmgroup([(op2[0][:, half * 512:(half + 1) * 512],
                                c.onesf_row[:, :],
                                srow[0:1, half * 512:(half + 1) * 512], True, True)
                               for half in range(C // 512)],
                              reads=[c.onesf_row, srow], writes=[op2[0]], tag="sbcmm")
                    p.copy("dve", sbc[:, :], op2[0][:, 0:C], tag="sbccp")
                for hb0 in range(0, HT, HB):
                    wcur = w13[(hb0 // HB) % 2]
                    w1src = c.sw1_bf if not routed else c.w11_bf[e]
                    w3src = c.sw3_bf if not routed else c.w31_bf[e]
                    hb = hb0 // HB
                    p.dma("sp",
                          wcur[:, :].rearrange("p (k dt h) -> p k dt h",
                                               k=2, h=HB * 128)[:, 0],
                          w1src[hb], tag="w1ld")
                    p.dma("sp",
                          wcur[:, :].rearrange("p (k dt h) -> p k dt h",
                                               k=2, h=HB * 128)[:, 1],
                          w3src[hb], tag="w3ld")
                    for hi in range(HB):
                        ht = hb0 + hi
                        mms1, mms3 = [], []
                        for dt in range(DT):
                            for half in range(C // 512):
                                lh1 = wcur[:, dt * HB * 128 + hi * 128:
                                           dt * HB * 128 + (hi + 1) * 128]
                                lh3 = wcur[:, DT * HB * 128 + dt * HB * 128 + hi * 128:
                                           DT * HB * 128 + dt * HB * 128 + (hi + 1) * 128]
                                rh = xnb[:, dt * C + half * 512: dt * C + (half + 1) * 512]
                                mms1.append((u1p[:, half * 512:(half + 1) * 512],
                                             lh1, rh, dt == 0, dt == DT - 1))
                                mms3.append((u3p[:, half * 512:(half + 1) * 512],
                                             lh3, rh, dt == 0, dt == DT - 1))
                        p.mmgroup(mms1, reads=[wcur, xnb], writes=[u1p], tag="u1mm")
                        p.mmgroup(mms3, reads=[wcur, xnb], writes=[u3p], tag="u3mm")
                        if routed:
                            p.tt("dve", t1[:, :], u1p[:, 0:C], sbc[:, :], ALU.mult,
                                 tag="t1")
                            p.act(ab[:, :], t1[:, :], AF.Silu, tag="sl1")
                            p.tt("dve", t3[:, :], u3p[:, 0:C], sbc[:, :], ALU.mult,
                                 tag="t3")
                            p.tt("dve", hsb[:, ht * C:(ht + 1) * C], ab[:, :], t3[:, :],
                                 ALU.mult, tag="h")
                        else:
                            p.act(ab[:, :], u1p[:, 0:C], AF.Silu, tag="sl1s")
                            p.tt("dve", hsb[:, ht * C:(ht + 1) * C], ab[:, :],
                                 u3p[:, 0:C], ALU.mult, tag="hs")
                for blk4 in range(4):
                    w2h, dh = blk4 // 2, blk4 % 2
                    wc = w2sb[blk4 % 2]
                    p.dma("sp", wc[:, :].rearrange("p (ht o) -> p ht o", o=1024),
                          w2src[w2h][:, :, dh * 1024:(dh + 1) * 1024], tag="w2ld")
                    for ot8 in range(8):
                        ot = dh * 8 + ot8
                        pp = op2[ot8 % 2]
                        mms = []
                        for j in range(7):
                            ht = w2h * 7 + j
                            for half in range(C // 512):
                                mms.append((pp[:, half * 512:(half + 1) * 512],
                                            wc[:, j * 1024 + ot8 * 128:
                                               j * 1024 + (ot8 + 1) * 128],
                                            hsb[:, ht * C + half * 512:
                                                ht * C + (half + 1) * 512],
                                            j == 0, j == 6))
                        p.mmgroup(mms, reads=[wc, hsb], writes=[pp], tag="w2mm")
                        osl = outb[:, ot * C:(ot + 1) * C]
                        if ei == 0 and w2h == 0:
                            p.copy("dve", osl, pp[:, 0:C], tag="ocp")
                        else:
                            p.tt("dve", osl, osl, pp[:, 0:C], ALU.add, tag="oadd")
            for q4 in range(4):
                p.dma("sp", x1t[:, :].rearrange("p (o t) -> p o t", t=C),
                      c.x1T.rearrange("(o p) t -> p o t", p=128)
                      [:, q4 * 4:(q4 + 1) * 4, col], tag="x1ld")
                p.dma("sp", at1t[:, :].rearrange("p (o t) -> p o t", t=C),
                      c.at1T.rearrange("(o p) t -> p o t", p=128)
                      [:, q4 * 4:(q4 + 1) * 4, col], tag="at1ld")
                p.tt("dve", x1t[:, :], x1t[:, :], at1t[:, :], ALU.add, tag="fa")
                p.tt("dve", x1t[:, :], x1t[:, :],
                     outb[:, q4 * 4 * C:(q4 + 1) * 4 * C], ALU.add, tag="fx")
                p.dma("pool", c.outT.rearrange("(o p) t -> p o t", p=128)
                      [:, q4 * 4:(q4 + 1) * 4, col],
                      x1t[:, :].rearrange("p (o t) -> p o t", t=C), tag="oout")
    p.barrier()


# version 14
# speedup vs baseline: 1.0132x; 1.0132x over previous
import numpy as np
import ml_dtypes
from contextlib import ExitStack

import concourse.bass as bass
import concourse.mybir as mybir
from concourse.bass_utils import run_bass_kernel_spmd

F32 = mybir.dt.float32
BF16 = mybir.dt.bfloat16
AF = mybir.ActivationFunctionType
ALU = mybir.AluOpType
AX = mybir.AxisListType

D = 2048
DT = 16
NH = 16
DH = 128
E = 8
HID = 1792
HT = 14
WIN = 512
NP = 16
NEG = -1e30
LN_EPS = 1e-5
RMS_EPS = 1e-6
SCALE0 = DH ** -0.5
SCALE1 = 1.0 / np.sqrt(2048.0)

ENGINES = ("pe", "dve", "act", "pool", "sp")
QUEUES = {"spq": "sp", "actq": "act", "poolq": "pool"}
R_SEMS = 8


def _apkey(ap):
    t = getattr(ap, "tensor", None)
    return t.name if t is not None else ap.name


class Prog:

    def __init__(self, nc):
        self.nc = nc
        self.ops = {e: [] for e in ENGINES}
        self.count = {e: 0 for e in ENGINES}
        self.qsem_count = {}
        self.qsem_rr = {q: 0 for q in QUEUES}
        self.doms = list(ENGINES) + [f"{q}{i}" for q in QUEUES for i in range(R_SEMS)]
        self.water = {e: {d: 0 for d in self.doms} for e in ENGINES}
        for q in QUEUES:
            for i in range(R_SEMS):
                self.qsem_count[f"{q}{i}"] = 0
        self.last_w = {}
        self.last_r = {}
        self.sems = {}
        self.n_mm = 0
        self.marks = []

    def mark(self, name):
        self.marks.append((name, self.n_mm))

    def init_sems(self, ctx):
        for d in self.doms:
            self.sems[d] = ctx.enter_context(self.nc.semaphore("sem_" + d))

    def _need(self, deps, dom, val):
        if val > 0:
            deps[dom] = max(deps.get(dom, 0), val)

    def _deps_for(self, reads, writes, extra_deps):
        deps = {}
        for ap in reads:
            w = self.last_w.get(_apkey(ap))
            if w:
                self._need(deps, w[0], w[1])
        for ap in writes:
            k = _apkey(ap)
            w = self.last_w.get(k)
            if w:
                self._need(deps, w[0], w[1])
            for d2, v in self.last_r.get(k, {}).items():
                self._need(deps, d2, v)
        for (d2, v) in extra_deps:
            self._need(deps, d2, v)
        return deps

    def _record(self, eng, dom, emit, reads, writes, deps, inc, tag):
        waits = []
        for d2, v in deps.items():
            if self.water[eng][d2] < v:
                waits.append((d2, v))
                self.water[eng][d2] = v
        if dom in self.qsem_count:
            myval = self.qsem_count[dom] + inc
            self.qsem_count[dom] = myval
        else:
            myval = self.count[dom] + inc
            self.count[dom] = myval
        self.ops[eng].append((waits, emit, dom, inc, tag))
        for ap in writes:
            k = _apkey(ap)
            self.last_w[k] = (dom, myval)
            self.last_r[k] = {}
        for ap in reads:
            d = self.last_r.setdefault(_apkey(ap), {})
            d[dom] = max(d.get(dom, 0), myval)
        return myval

    def op(self, eng, emit, reads=(), writes=(), tag="", extra_deps=()):
        deps = self._deps_for(reads, writes, extra_deps)
        return self._record(eng, eng, emit, reads, writes, deps, 1, tag)

    def dma(self, queue, dst, src, tag="dma", **kw):
        queue = {"sp": "spq", "act": "actq", "pool": "poolq"}.get(queue, queue)
        eng = QUEUES[queue]
        i = self.qsem_rr[queue]
        self.qsem_rr[queue] = (i + 1) % R_SEMS
        dom = f"{queue}{i}"
        deps = self._deps_for([src], [dst], ())
        prior = self.qsem_count[dom]
        if prior:
            self._need(deps, dom, prior)
        emit = lambda e: e.dma_start(dst, src, **kw)
        return self._record(eng, dom, emit, [src], [dst], deps, 16, tag)

    def barrier(self, skip_prefix=None):
        snap = dict(self.count)
        snap.update(self.qsem_count)
        doms = [d for d in self.doms
                if skip_prefix is None or not d.startswith(skip_prefix)]
        for e in ENGINES:
            waits = []
            for d2 in doms:
                if d2 != e and snap[d2] > self.water[e][d2]:
                    waits.append((d2, snap[d2]))
            if waits:
                self.ops[e].append((waits, False, e, 0, "barrier"))
            for d2 in doms:
                self.water[e][d2] = max(self.water[e][d2], snap[d2])
        if skip_prefix is None:
            self.last_w = {}
            self.last_r = {}
        else:
            self.last_w = {k: v for k, v in self.last_w.items()
                           if v[0].startswith(skip_prefix)}
            self.last_r = {k: {d: v for d, v in rd.items()
                               if d.startswith(skip_prefix)}
                           for k, rd in self.last_r.items()}
            self.last_r = {k: rd for k, rd in self.last_r.items() if rd}

    def emit(self):
        nc = self.nc
        sems = self.sems
        with nc.Block() as blk:
            def run(eng_obj, eng_name):
                for waits, emit, dom, inc, tag in self.ops[eng_name]:
                    for (d2, v) in waits:
                        eng_obj.wait_ge(sems[d2], v)
                    if emit is False:
                        continue
                    emit(eng_obj).then_inc(sems[dom], inc)
            if self.ops["sp"]:
                blk.sync(lambda e: run(e, "sp"))
            if self.ops["pe"]:
                blk.tensor(lambda e: run(e, "pe"))
            if self.ops["dve"]:
                blk.vector(lambda e: run(e, "dve"))
            if self.ops["act"]:
                blk.scalar(lambda e: run(e, "act"))
            if self.ops["pool"]:
                blk.gpsimd(lambda e: run(e, "pool"))

    def mmgroup(self, mms, reads, writes, tag="mm"):
        self.n_mm += len(mms) + sum(1 for m in mms if False)
        def emit(e):
            last = None
            for mm in mms:
                out, lh, rh, st, sp = mm[:5]
                last = e.matmul(out, lh, rh, start=st, stop=sp)
            return last
        return self.op("pe", emit, reads=reads, writes=writes, tag=tag)

    def tt(self, eng, out, a, b, alu, tag="tt"):
        return self.op(eng, lambda e: e.tensor_tensor(out, a, b, alu),
                       reads=[a, b], writes=[out], tag=tag)

    def ts(self, eng, out, a, s1, alu, s2=None, alu2=None, tag="ts"):
        sr = [x for x in (s1, s2) if hasattr(x, "tensor")]
        if alu2 is None:
            return self.op(eng, lambda e: e.tensor_single_scalar(out, a, s1, alu),
                           reads=[a] + sr, writes=[out], tag=tag)
        return self.op(eng, lambda e: e.tensor_scalar(out, a, s1, s2, alu, alu2),
                       reads=[a] + sr, writes=[out], tag=tag)

    def stt(self, eng, out, in0, scal, in1, alu0, alu1, tag="stt"):
        sr = [scal] if hasattr(scal, "tensor") else []
        return self.op(eng, lambda e: e.scalar_tensor_tensor(out, in0, scal, in1, alu0, alu1),
                       reads=[in0, in1] + sr, writes=[out], tag=tag)

    def act(self, out, in_, func, bias=0.0, scale=1.0, accum=None, tag="act"):
        br = [bias] if hasattr(bias, "tensor") else []
        sr = [scale] if hasattr(scale, "tensor") else []
        wr = [out] + ([accum] if accum is not None else [])
        if accum is None:
            return self.op("act", lambda e: e.activation(out, in_, func, bias=bias, scale=scale),
                           reads=[in_] + br + sr, writes=wr, tag=tag)
        return self.op("act", lambda e: e.activation(out, in_, func, bias=bias, scale=scale,
                                                     accum_out=accum),
                       reads=[in_] + br + sr, writes=wr, tag=tag)

    def copy(self, eng, out, in_, tag="cp"):
        if eng == "act":
            return self.op("act", lambda e: e.copy(out, in_), reads=[in_], writes=[out], tag=tag)
        return self.op(eng, lambda e: e.tensor_copy(out, in_), reads=[in_], writes=[out], tag=tag)

    def red(self, out, in_, alu, negate=False, tag="red"):
        return self.op("dve", lambda e: e.tensor_reduce(out, in_, AX.X, alu, negate=negate),
                       reads=[in_], writes=[out], tag=tag)

    def recip(self, out, in_, tag="recip"):
        return self.op("dve", lambda e: e.reciprocal(out, in_), reads=[in_], writes=[out], tag=tag)

    def transp_mark(self):
        pass

    def transp(self, out_ps, in_sb, ident, tag="tr"):
        self.n_mm += 1
        return self.op("pe", lambda e: e.transpose(out_ps, in_sb, ident),
                       reads=[in_sb, ident], writes=[out_ps], tag=tag)



class Ctx:
    pass


def declare_io(nc, T):
    c = Ctx()
    c.T = T
    di = lambda n, s, dt=F32: nc.dram_tensor(n, s, dt, kind="ExternalInput").ap()
    ds = lambda n, s, dt: nc.dram_tensor(n, s, dt, kind="Internal").ap()
    c.xT = di("xT", [D, T])
    c.wqkv0T = di("wqkv0T", [6, 128, DT, 1024])
    c.wout0T = di("wout0T", [128, DT, D])
    c.pm0T = di("pm0T", [NH, DH, NP])
    c.pm1 = di("pm1", [NH, NP, DH])
    c.anw0c = di("anw0c", [128, DT])
    c.rw0 = di("rw0", [D, E])
    c.rb0 = di("rb0", [1, E])
    c.ew0 = di("ew0", [E, 4, 128, DT, 512])
    c.eb0c = di("eb0c", [128, E * DT])
    c.pw0 = di("pw0", [4, 128, DT, 512])
    c.pb0c = di("pb0c", [128, DT])
    c.fnw0c = di("fnw0c", [128, DT])
    c.wqkv1T = di("wqkv1T", [6, 128, DT, 1024])
    c.bqk1c = di("bqk1c", [128, 2 * DT])
    c.bv1 = di("bv1", [1, D])
    c.wout1T = di("wout1T", [128, DT, D])
    c.bo1c = di("bo1c", [128, DT])
    c.gw1 = di("gw1", [D, E])
    c.ebias1 = di("ebias1", [1, E])
    c.w11 = di("w11", [E, 7, 128, DT, 256])
    c.w21 = di("w21", [E, 2, 128, 7, D])
    c.w31 = di("w31", [E, 7, 128, DT, 256])
    c.sw1 = di("sw1", [7, 128, DT, 256])
    c.sw2 = di("sw2", [2, 128, 7, D])
    c.sw3 = di("sw3", [7, 128, DT, 256])
    c.identf = di("identf", [128, 128])
    c.identb = di("identb", [128, 128], BF16)
    c.swapT = di("swapT", [128, 128], BF16)
    c.cosT = di("cosT", [128, T], BF16)
    c.sinT = di("sinT", [128, T], BF16)
    c.maskW = di("maskW", [4, 128, WIN + NP], BF16)
    c.pmk_bf = ds("pmk_bf", [D, NP], BF16)
    c.wqkv0T_bf = ds("wqkv0T_bf", [6, 128, DT, 1024], BF16)
    c.wout0T_bf = ds("wout0T_bf", [128, DT, D], BF16)
    c.ew0_bf = ds("ew0_bf", [E, 4, 128, DT, 512], BF16)
    c.pw0_bf = ds("pw0_bf", [4, 128, DT, 512], BF16)
    c.wqkv1T_bf = ds("wqkv1T_bf", [6, 128, DT, 1024], BF16)
    c.wout1T_bf = ds("wout1T_bf", [128, DT, D], BF16)
    c.w11_bf = ds("w11_bf", [E, 7, 128, DT, 256], BF16)
    c.w21_bf = ds("w21_bf", [E, 2, 128, 7, D], BF16)
    c.w31_bf = ds("w31_bf", [E, 7, 128, DT, 256], BF16)
    c.sw1_bf = ds("sw1_bf", [7, 128, DT, 256], BF16)
    c.sw2_bf = ds("sw2_bf", [2, 128, 7, D], BF16)
    c.sw3_bf = ds("sw3_bf", [7, 128, DT, 256], BF16)
    c.xn0T = ds("xn0T", [D, T], BF16)
    c.xa0T = ds("xa0T", [D, T], BF16)
    c.q0T = ds("q0T", [D, T], BF16)
    c.k0T = ds("k0T", [D, T], BF16)
    c.v0 = ds("v0", [T, D], BF16)
    c.at0T = ds("at0T", [D, T], BF16)
    c.w0rT = ds("w0rT", [E, T], F32)
    c.x1T = ds("x1T", [D, T], F32)
    c.xn1T = ds("xn1T", [D, T], BF16)
    c.s1rT = ds("s1rT", [E, T], F32)
    c.q1T = ds("q1T", [D, T], BF16)
    c.k1T = ds("k1T", [D, T], BF16)
    c.v1 = ds("v1", [T, D], BF16)
    c.ctx1T = ds("ctx1T", [D, T], BF16)
    c.at1T = ds("at1T", [D, T], BF16)
    c.outT = nc.dram_tensor("outT", [D, T], F32, kind="ExternalOutput").ap()
    return c


def load_persistent(nc, p, c, ctx):
    sb = lambda n, s, dt=F32: ctx.enter_context(nc.sbuf_tensor(n, s, dt))
    c.identf_sb = sb("identf_sb", [128, 128])
    c.identb_sb = sb("identb_sb", [128, 128], BF16)
    c.swap_sb = sb("swap_sb", [128, 128], BF16)
    c.onesf_col = sb("onesf_col", [128, 1])
    c.onesb_col = sb("onesb_col", [128, 1], BF16)
    c.onesf_row = sb("onesf_row", [1, 128])
    c.oneone = sb("oneone", [1, 1])
    c.anw0_sb = sb("anw0_sb", [128, DT])
    c.eb0_sb = sb("eb0_sb", [128, E * DT])
    c.pb0_sb = sb("pb0_sb", [128, DT])
    c.fnw0_sb = sb("fnw0_sb", [128, DT])
    c.bqk1_sb = sb("bqk1_sb", [128, 2 * DT])
    c.bo1_sb = sb("bo1_sb", [128, DT])
    c.bv1_sb = sb("bv1_sb", [1, D])
    c.rw0_sb = sb("rw0_sb", [128, DT * E])
    c.gw1_sb = sb("gw1_sb", [128, DT * E])
    c.rb0_sb = sb("rb0_sb", [1, E])
    c.eb1r_sb = sb("eb1r_sb", [1, E])
    c.ebc1_sb = sb("ebc1_sb", [128, E])
    c.pm0_sb = sb("pm0_sb", [128, NH * NP], BF16)
    c.eps_ln = sb("eps_ln", [128, 1])
    c.eps_rms = sb("eps_rms", [128, 1])
    c.bqk1s_sb = sb("bqk1s_sb", [128, DT])
    c.pm1_sb = sb("pm1_sb", [16, NH * DH], BF16)

    p.dma("sp", c.identf_sb[:, :], c.identf)
    p.dma("sp", c.identb_sb[:, :], c.identb)
    p.dma("sp", c.swap_sb[:, :], c.swapT)
    p.dma("sp", c.anw0_sb[:, :], c.anw0c)
    p.dma("sp", c.eb0_sb[:, :], c.eb0c)
    p.dma("sp", c.pb0_sb[:, :], c.pb0c)
    p.dma("sp", c.fnw0_sb[:, :], c.fnw0c)
    p.dma("sp", c.bqk1_sb[:, :], c.bqk1c)
    p.dma("sp", c.bo1_sb[:, :], c.bo1c)
    p.dma("sp", c.bv1_sb[:, :], c.bv1)
    p.dma("sp", c.rw0_sb[:, :].rearrange("p (dt e) -> p dt e", e=E),
          c.rw0.rearrange("(dt p) e -> p dt e", p=128))
    p.dma("sp", c.gw1_sb[:, :].rearrange("p (dt e) -> p dt e", e=E),
          c.gw1.rearrange("(dt p) e -> p dt e", p=128))
    p.dma("sp", c.rb0_sb[:, :], c.rb0)
    p.dma("sp", c.eb1r_sb[:, :], c.ebias1)
    p.dma("pool", c.pm0_sb[:, :].rearrange("p (h n) -> p h n", n=NP),
          c.pm0T.rearrange("h p n -> p h n"))
    p.dma("pool", c.pm1_sb[:, :].rearrange("n (h d) -> n h d", d=DH),
          c.pm1.rearrange("h n d -> n h d"))
    for t, v in [(c.onesf_col, 1.0), (c.onesb_col, 1.0), (c.onesf_row, 1.0),
                 (c.oneone, 1.0), (c.eps_ln, LN_EPS), (c.eps_rms, RMS_EPS)]:
        p.op("pool", lambda e, t=t, v=v: e.memset(t[:, :], v), writes=[t])
    p.dma("sp", c.pmk_bf.rearrange("(h p) n -> p h n", p=128),
          c.pm0_sb[:, :].rearrange("p (h n) -> p h n", n=NP), tag="pmkout")
    p.ts("dve", c.bqk1s_sb[:, :], c.bqk1_sb[:, 0:DT], SCALE1, ALU.mult, tag="bqs")
    with nc.psum_tensor("ebc_ps", [128, 512], F32) as ebc_ps:
        p.mmgroup([(ebc_ps[:, 0:E], c.onesf_row[:, :], c.eb1r_sb[:, :], True, True)],
                  reads=[c.onesf_row, c.eb1r_sb], writes=[ebc_ps], tag="ebc")
        p.copy("dve", c.ebc1_sb[:, :], ebc_ps[:, 0:E])
    p.barrier()


def phase_preconvert(nc, p, c, ctx, part):
    width = 6144 if part == 1 else 2048
    nbuf = 2 if part == 1 else 1
    bb = [ctx.enter_context(nc.sbuf_tensor(f"pcb{part}{i}", [128, width], BF16))
          for i in range(nbuf)]
    if part == 1:
        jobs = ([(c.wqkv0T, c.wqkv0T_bf), (c.wout0T, c.wout0T_bf)]
                + [(c.ew0[e], c.ew0_bf[e]) for e in range(E)]
                + [(c.pw0, c.pw0_bf), (c.wqkv1T, c.wqkv1T_bf),
                   (c.wout1T, c.wout1T_bf)])
    else:
        jobs = ([(c.sw1, c.sw1_bf), (c.sw3, c.sw3_bf), (c.sw2, c.sw2_bf)]
                + [(c.w11[e], c.w11_bf[e]) for e in range(E)]
                + [(c.w31[e], c.w31_bf[e]) for e in range(E)]
                + [(c.w21[e], c.w21_bf[e]) for e in range(E)])
    k = 0
    dims = "abcde"
    for srcap, dstap in jobs:
        L = 1
        for s in srcap.shape:
            L *= s
        spec = " ".join(dims[:len(srcap.shape)])
        sflat = srcap.rearrange(f"{spec} -> ({spec})").rearrange("(q y) -> q y", q=128)
        dflat = dstap.rearrange(f"{spec} -> ({spec})").rearrange("(q y) -> q y", q=128)
        Y = L // 128
        for y0 in range(0, Y, width):
            y1 = min(y0 + width, Y)
            buf = bb[k % nbuf]
            k += 1
            p.dma("pool", buf[:, 0:y1 - y0], sflat[:, y0:y1], tag="pcin")
            p.dma("pool", dflat[:, y0:y1], buf[:, 0:y1 - y0], tag="pcout")


def phase_ln_router(nc, p, c, layer, skip_prefix=None):
    T = c.T
    C = 512
    NSUB = C // 128
    x_src = c.xT if layer == 0 else c.x1T
    xn_dst = c.xn0T if layer == 0 else c.xn1T
    wr_dst = c.w0rT if layer == 0 else c.s1rT
    rw_sb = c.rw0_sb if layer == 0 else c.gw1_sb

    with ExitStack() as ctx:
        sb = lambda n, s, dt=F32: ctx.enter_context(nc.sbuf_tensor(f"L{layer}{n}", s, dt))
        ps = lambda n, s, dt=F32: ctx.enter_context(nc.psum_tensor(f"L{layer}{n}", s, dt))
        x32 = [sb(f"lr_x32_{i}", [128, DT * C]) for i in range(2)]
        xnf = sb("lr_xnf", [128, DT * C])
        xnb = sb("lr_xnb", [128, DT * C], BF16)
        xab = sb("lr_xab", [128, DT * C], BF16) if layer == 0 else None
        xsqb = sb("lr_xsqb", [128, DT * C], BF16)
        rows = sb("lr_rows", [1, 7 * C])
        lg = sb("lr_lg", [128, 8 * NSUB])
        m1 = sb("lr_m1", [128, NSUB])
        m2 = sb("lr_m2", [128, NSUB])
        eq = sb("lr_eq", [128, 8])
        msk = sb("lr_msk", [128, 8])
        ind = sb("lr_ind", [128, 8])
        sb_sel = sb("lr_sbsel", [128, 8])
        nm1 = sb("lr_nm1", [128, NSUB])
        ex = sb("lr_ex", [128, 8])
        e2 = sb("lr_e2", [128, 1])
        den = sb("lr_den", [128, 1])
        rec = sb("lr_rec", [128, NSUB])
        wts = sb("lr_wts", [128, 8])
        wrow = sb("lr_wrow", [8, 128])
        sum_ps = ps("lr_sum_ps", [1, 512])
        sum2_ps = ps("lr_sum2_ps", [1, 512])
        bc_ps = ps("lr_bc_ps", [128, C])
        bc2_ps = ps("lr_bc2_ps", [128, C])
        lg_ps = ps("lr_lg_ps", [128, 512])
        tr_ps = ps("lr_tr_ps", [128, 512])
        r2_ps = ps("lr_r2_ps", [128, C]) if layer == 0 else None

        def row(i, n=C):
            return rows[0:1, i * C:i * C + n]

        for ch in range(T // C):
            xb = x32[ch % 2]
            col = slice(ch * C, (ch + 1) * C)
            for h in range(2):
                sl = slice(h * 8, (h + 1) * 8)
                p.dma("sp", xb[:, :].rearrange("p (dt c) -> p dt c", c=C)[:, sl],
                      x_src.rearrange("(dt p) t -> p dt t", p=128)[:, sl, col],
                      tag=f"lrx{ch}")
            for dt in range(DT):
                xt = xb[:, dt * C:(dt + 1) * C]
                p.act(xsqb[:, dt * C:(dt + 1) * C], xt, AF.Square, tag="sq")
            p.mmgroup([(sum_ps[0:1, 0:C], c.onesf_col[:, :], xb[:, dt * C:(dt + 1) * C],
                        dt == 0, dt == DT - 1) for dt in range(DT)],
                      reads=[c.onesf_col, xb], writes=[sum_ps], tag="redx")
            p.mmgroup([(sum2_ps[0:1, 0:C], c.onesb_col[:, :], xsqb[:, dt * C:(dt + 1) * C],
                        dt == 0, dt == DT - 1) for dt in range(DT)],
                      reads=[c.onesb_col, xsqb], writes=[sum2_ps], tag="redx2")
            p.ts("dve", row(0), sum_ps[0:1, 0:C], 1.0 / D, ALU.mult, tag="m")
            p.ts("dve", row(1), sum2_ps[0:1, 0:C], 1.0 / D, ALU.mult, tag="q")
            p.tt("dve", row(2), row(0), row(0), ALU.mult, tag="mm")
            p.tt("dve", row(2), row(1), row(2), ALU.subtract, tag="var")
            p.act(row(3), row(2), AF.Ln, bias=c.eps_ln[0:1, :], tag="lnv")
            p.act(row(3), row(3), AF.Exp, scale=-0.5, tag="rstd")
            p.tt("dve", row(4), row(0), row(3), ALU.mult, tag="mr")
            p.mmgroup([(bc_ps[:, :], c.onesf_row[:, :], row(3), True, True)],
                      reads=[c.onesf_row, rows], writes=[bc_ps], tag="bcR")
            p.mmgroup([(bc2_ps[:, :], c.onesf_row[:, :], row(4), True, True)],
                      reads=[c.onesf_row, rows], writes=[bc2_ps], tag="bcMR")
            for dt in range(DT):
                xt = xb[:, dt * C:(dt + 1) * C]
                xnfs = xnf[:, dt * C:(dt + 1) * C]
                p.tt("dve", xnfs, xt, bc_ps[:, :], ALU.mult, tag="xr")
                p.tt("dve", xnfs, xnfs, bc2_ps[:, :], ALU.subtract, tag="xn")
                p.copy("act", xnb[:, dt * C:(dt + 1) * C],
                       xnf[:, dt * C:(dt + 1) * C], tag="xnb")
            for s in range(NSUB):
                lgp = lg_ps[:, s * 8:s * 8 + 8]
                mms = [(lgp, c.onesf_row[:, :], c.rb0_sb[:, :], True, False)] if layer == 0 \
                    else []
                st0 = layer != 0
                for dt in range(DT):
                    mms.append((lgp, xnf[:, dt * C + s * 128: dt * C + s * 128 + 128],
                                rw_sb[:, :].rearrange("p (dt e) -> p dt e", e=E)[:, dt],
                                st0 and dt == 0, dt == DT - 1))
                p.mmgroup(mms, reads=[xnf, rw_sb] + ([c.rb0_sb] if layer == 0 else []),
                          writes=[lg_ps], tag=f"lg{s}")
                lgs = lg[:, s * 8:s * 8 + 8]
                m1s = m1[:, s:s + 1]
                if layer == 0:
                    p.copy("dve", lgs, lgp, tag="lgcp")
                else:
                    p.red(m1s, lgp, ALU.max, tag="gm")
                    p.ts("dve", nm1[:, s:s + 1], m1s, -1.0, ALU.mult, tag="gnm")
                    p.act(ex[:, :], lgp, AF.Exp, bias=nm1[:, s:s + 1],
                          accum=den[:, :], tag="gex")
                    p.recip(rec[:, s:s + 1], den[:, :], tag="grec")
                    p.ts("dve", lgs, ex[:, :], rec[:, s:s + 1], ALU.mult, tag="gsc")
                if layer == 1:
                    p.tt("dve", sb_sel[:, :], lgs, c.ebc1_sb[:, :], ALU.add, tag="gsb")
                    selap = sb_sel[:, :]
                else:
                    selap = lgs
                p.red(m1s, selap, ALU.max, tag="m1")
                p.ts("dve", eq[:, :], selap, m1s, ALU.is_ge, tag="eq")
                p.stt("dve", msk[:, :], eq[:, :], NEG, selap, ALU.mult, ALU.add, tag="msk")
                p.red(m2[:, s:s + 1], msk[:, :], ALU.max, tag="m2")
                p.ts("dve", ind[:, :], selap, m2[:, s:s + 1], ALU.is_ge, tag="ind")
                if layer == 0:
                    p.ts("dve", nm1[:, s:s + 1], m1s, -1.0, ALU.mult, tag="nm1")
                    p.act(ex[:, :], lgs, AF.Exp, bias=nm1[:, s:s + 1], tag="exl")
                    p.act(e2[:, :], m2[:, s:s + 1], AF.Exp, bias=nm1[:, s:s + 1], tag="ex2")
                    p.ts("dve", den[:, :], e2[:, :], 1.0, ALU.add, tag="den")
                    p.recip(rec[:, s:s + 1], den[:, :], tag="rec")
                    p.stt("dve", wts[:, :], ex[:, :], rec[:, s:s + 1], ind[:, :],
                          ALU.mult, ALU.mult, tag="wts")
                else:
                    p.tt("dve", wts[:, :], lgs, ind[:, :], ALU.mult, tag="wts")
                p.transp(tr_ps[0:8, 0:128], wts[:, :], c.identf_sb[:, :], tag="wtr")
                p.copy("dve", wrow[:, :], tr_ps[0:8, 0:128], tag="wrowcp")
                p.dma("sp", wr_dst[:, ch * C + s * 128: ch * C + (s + 1) * 128],
                      wrow[:, :], tag="wrow")
            if layer == 0:
                for dt in range(DT):
                    p.act(xsqb[:, dt * C:(dt + 1) * C], xnb[:, dt * C:(dt + 1) * C],
                          AF.Square, tag="sq2")
                p.mmgroup([(sum2_ps[0:1, 0:C], c.onesb_col[:, :],
                            xsqb[:, dt * C:(dt + 1) * C], dt == 0, dt == DT - 1)
                           for dt in range(DT)],
                          reads=[c.onesb_col, xsqb], writes=[sum2_ps], tag="redn2")
                p.ts("dve", row(5), sum2_ps[0:1, 0:C], 1.0 / D, ALU.mult, tag="q2")
                p.act(row(6), row(5), AF.Ln, bias=c.eps_rms[0:1, :], tag="lnv2")
                p.act(row(6), row(6), AF.Exp, scale=-0.5, tag="rstd2")
                p.mmgroup([(r2_ps[:, :], c.onesf_row[:, :], row(6), True, True)],
                          reads=[c.onesf_row, rows], writes=[r2_ps], tag="bcR2")
                for dt in range(DT):
                    p.stt("dve", xab[:, dt * C:(dt + 1) * C], xnb[:, dt * C:(dt + 1) * C],
                          c.anw0_sb[:, dt:dt + 1], r2_ps[:, :], ALU.mult, ALU.mult,
                          tag="xa")
            for h in range(2):
                sl = slice(h * 8, (h + 1) * 8)
                p.dma("sp", xn_dst.rearrange("(dt p) t -> p dt t", p=128)[:, sl, col],
                      xnb[:, :].rearrange("p (dt c) -> p dt c", c=C)[:, sl], tag="xnsp")
                if layer == 0:
                    p.dma("sp", c.xa0T.rearrange("(dt p) t -> p dt t", p=128)[:, sl, col],
                          xab[:, :].rearrange("p (dt c) -> p dt c", c=C)[:, sl], tag="xasp")
    p.barrier(skip_prefix=skip_prefix)


def build_program(T=4096, phases=("ln0", "qkv0", "attn0", "moe0",
                                  "ln1", "qkv1", "attn1", "wout1", "moe1")):
    nc = bass.Bass("TRN2", target_bir_lowering=False)
    c = declare_io(nc, T)
    p = Prog(nc)
    with ExitStack() as ctx:
        p.init_sems(ctx)
        load_persistent(nc, p, c, ctx)
        with ExitStack() as pcctx:
            phase_preconvert(nc, p, c, pcctx, 1)
            if "ln0" in phases:
                p.mark("ln0")
                phase_ln_router(nc, p, c, 0, skip_prefix="poolq")
            if "qkv0" in phases:
                p.mark("qkv0")
                phase_qkv(nc, p, c, 0, skip_prefix="poolq")
            p.barrier()
        if "attn0" in phases:
            p.mark("attn0")
            phase_attn0(nc, p, c)
        with ExitStack() as pcctx2:
            phase_preconvert(nc, p, c, pcctx2, 2)
            if "moe0" in phases:
                p.mark("moe0")
                phase_moe0(nc, p, c, skip_prefix="poolq")
            if "ln1" in phases:
                p.mark("ln1")
                phase_ln_router(nc, p, c, 1, skip_prefix="poolq")
            if "qkv1" in phases:
                p.mark("qkv1")
                phase_qkv(nc, p, c, 1, skip_prefix="poolq")
            p.barrier()
        if "attn1" in phases:
            p.mark("attn1")
            phase_attn1(nc, p, c)
        if "wout1" in phases:
            p.mark("wout1")
            phase_wout1(nc, p, c)
        if "moe1" in phases:
            p.mark("moe1")
            phase_moe1(nc, p, c)
        p.mark("end")
        p.barrier()
        p.emit()
    nc._marks = p.marks
    return nc



def _bf(x):
    return np.asarray(x, dtype=ml_dtypes.bfloat16)


def host_constants(T):
    bf = ml_dtypes.bfloat16
    ident = np.eye(128, dtype=np.float32)
    S = np.zeros((128, 128), dtype=np.float32)
    for i in range(64):
        S[2 * i + 1, 2 * i] = -1.0
        S[2 * i, 2 * i + 1] = 1.0
    inv_freq = 1.0 / (10000.0 ** (np.arange(0, DH, 2, dtype=np.float32) / DH))
    pos = np.arange(T, dtype=np.float32)
    fr = pos[None, :] * inv_freq[:, None]
    fr2 = np.repeat(fr, 2, axis=0)
    cosT = np.cos(fr2)
    sinT = np.sin(fr2)
    q_len, k_len, pq, pk = WIN, WIN + NP, NP, 2 * NP
    bias = np.full((q_len, k_len), NEG, dtype=np.float32)
    bias[:, :pk] = 0.0
    sq = q_len - pq
    causal = np.tril(np.ones((sq, sq), dtype=bool))
    bias[pq:, pk:] = np.where(causal, 0.0, NEG)
    maskW = np.concatenate([np.zeros((q_len, NP), np.float32), bias[:, NP:]],
                           axis=1).reshape(4, 128, WIN + NP)
    return {
        "identf": ident, "identb": _bf(ident), "swapT": _bf(S),
        "cosT": _bf(cosT), "sinT": _bf(sinT), "maskW": _bf(maskW * SCALE0),
    }


def host_inputs(inp, T):
    f32 = lambda a: np.ascontiguousarray(np.asarray(a, dtype=np.float32))
    col = lambda a: f32(np.asarray(a).reshape(-1, 128).T)
    def qkvL(w):
        return f32(np.asarray(w).T.reshape(DT, 128, 6, 1024).transpose(2, 1, 0, 3))

    def outL(w):
        return f32(np.asarray(w).T.reshape(DT, 128, D).transpose(1, 0, 2))

    def ewL(w):
        F = np.asarray(w).shape[1]
        return f32(np.asarray(w).reshape(DT, 128, F // 512, 512).transpose(2, 1, 0, 3))

    def hbL(w):
        return f32(np.asarray(w).reshape(DT, 128, 7, 256).transpose(2, 1, 0, 3))

    def w2L(w):
        return f32(np.asarray(w).reshape(2, 7, 128, D).transpose(0, 2, 1, 3))

    d = {
        "wqkv0T": qkvL(inp["L0_attn_w_qkv"]),
        "wout0T": outL(inp["L0_attn_w_out"]),
        "pm0T": f32(np.asarray(inp["L0_attn_pm"])[0].transpose(0, 2, 1)),
        "pm1": f32(np.asarray(inp["L0_attn_pm"])[1]),
        "anw0c": col(inp["L0_attn_norm_w"]),
        "rw0": f32(inp["L0_ffn_router_w"]),
        "rb0": f32(np.asarray(inp["L0_ffn_router_b"]).reshape(1, E)),
        "ew0": np.stack([ewL(w) for w in np.asarray(inp["L0_ffn_expert_w"], dtype=np.float32)]),
        "eb0c": f32(np.asarray(inp["L0_ffn_expert_b"]).reshape(E * DT, 128).T),
        "pw0": ewL(inp["L0_ffn_proj_w"]),
        "pb0c": col(inp["L0_ffn_proj_b"]),
        "fnw0c": col(inp["L0_ffn_norm_w"]),
        "wqkv1T": qkvL(inp["L1_attn_w_qkv"]),
        "bqk1c": col(np.asarray(inp["L1_attn_b_qkv"])[:2 * D]),
        "bv1": f32(np.asarray(inp["L1_attn_b_qkv"])[2 * D:].reshape(1, D)),
        "wout1T": outL(inp["L1_attn_w_out"]),
        "bo1c": col(inp["L1_attn_b_out"]),
        "gw1": f32(inp["L1_ffn_gate_w"]),
        "ebias1": f32(np.asarray(inp["L1_ffn_expert_bias"]).reshape(1, E)),
        "w11": np.stack([hbL(w) for w in np.asarray(inp["L1_ffn_w1"], dtype=np.float32)]),
        "w21": np.stack([w2L(w) for w in np.asarray(inp["L1_ffn_w2"], dtype=np.float32)]),
        "w31": np.stack([hbL(w) for w in np.asarray(inp["L1_ffn_w3"], dtype=np.float32)]),
        "sw1": hbL(inp["L1_ffn_sw1"]),
        "sw2": w2L(inp["L1_ffn_sw2"]),
        "sw3": hbL(inp["L1_ffn_sw3"]),
    }
    d.update(host_constants(T))
    return d


def kernel(**inputs):
    T = 4096
    x = np.asarray(inputs["L0_attn_x"], dtype=np.float32)
    shared = host_inputs(inputs, T)
    nc = build_program(T)
    in_maps = []
    for i in range(8):
        m = dict(shared)
        m["xT"] = np.ascontiguousarray(x[i].T)
        in_maps.append(m)
    res = run_bass_kernel_spmd(nc, in_maps, core_ids=list(range(8)))
    out = np.stack([np.asarray(r["outT"]).T for r in res.results], axis=0)
    return np.ascontiguousarray(out)


def phase_qkv(nc, p, c, layer, skip_prefix=None):
    T = c.T
    TC = min(1024, T)
    NT = T // TC
    wsrc = c.wqkv0T_bf if layer == 0 else c.wqkv1T_bf
    asrc = c.xa0T if layer == 0 else c.xn1T
    qdst = c.q0T if layer == 0 else c.q1T
    kdst = c.k0T if layer == 0 else c.k1T
    vdst = c.v0 if layer == 0 else c.v1

    with ExitStack() as ctx:
        sb = lambda n, s, dt=F32: ctx.enter_context(nc.sbuf_tensor(f"qk{layer}{n}", s, dt))
        ps = lambda n, s: ctx.enter_context(nc.psum_tensor(f"qk{layer}{n}", s, F32))
        wb = [sb(f"w{i}", [128, DT * 1024], BF16) for i in range(2)]
        ab = [sb(f"a{i}", [128, DT * TC], BF16) for i in range(2)]
        qr = [sb(f"qr{i}", [128, TC], BF16) for i in range(2)]
        t1 = [sb(f"t1{i}", [128, TC], BF16) for i in range(2)]
        t2 = [sb(f"t2{i}", [128, TC], BF16) for i in range(2)]
        qro = [sb(f"qro{i}", [128, TC], BF16) for i in range(2)]
        vo = [sb(f"vo{i}", [128, 1024], BF16) for i in range(2)]
        if layer == 0:
            cos_sb = sb("cos", [128, T], BF16)
            sin_sb = sb("sin", [128, T], BF16)
            p.dma("sp", cos_sb[:, :], c.cosT)
            p.dma("sp", sin_sb[:, :], c.sinT)
        pp = [ps(f"pp{i}", [128, 1024]) for i in range(2)]
        rp = [ps(f"rp{i}", [128, TC]) for i in range(2)]

        def wtile(b, dt, o0, n):
            return wb[b][:, dt * 1024 + o0: dt * 1024 + o0 + n]

        def atile(b, dt, t0, n):
            return ab[b][:, dt * TC + t0: dt * TC + t0 + n]

        for ob in range(6):
            wcur = wb[ob % 2]
            p.dma("sp", wcur[:, :].rearrange("p (dt o) -> p dt o", o=1024),
                  wsrc[ob], tag=f"qw{ob}")
            for tch in range(NT):
                acur = ab[tch % 2] if NT > 1 else ab[0]
                if ob == 0 or NT > 1:
                    for h in range(2):
                        sl = slice(h * 8, (h + 1) * 8)
                        p.dma("sp", acur[:, :].rearrange("p (dt t) -> p dt t", t=TC)[:, sl],
                              asrc.rearrange("(dt p) t -> p dt t", p=128)
                              [:, sl, tch * TC:(tch + 1) * TC], tag=f"qa{tch}")
                is_v = ob >= 4
                if not is_v:
                    for ot in range(8):
                        o_glob = ob * 1024 + ot * 128
                        qp = pp[ot % 2]
                        mms = []
                        for dt in range(DT):
                            for half in range(TC // 512):
                                mms.append((qp[:, half * 512:(half + 1) * 512],
                                            wtile(ob % 2, dt, ot * 128, 128),
                                            atile(tch % 2 if NT > 1 else 0, dt,
                                                  half * 512, 512),
                                            dt == 0, dt == DT - 1))
                        p.mmgroup(mms, reads=[wcur, acur], writes=[qp], tag=f"qkmm{ob}")
                        pr = ot % 2
                        if layer == 0:
                            qsc = SCALE0 if o_glob < D else 1.0
                            p.op("act", lambda e, qp=qp, pr=pr, qsc=qsc:
                                 e.activation(qr[pr][:, :], qp[:, 0:TC], AF.Copy,
                                              scale=qsc),
                                 reads=[qp], writes=[qr[pr]], tag="qevac")
                            rmm = [(rp[pr][:, half * 512:(half + 1) * 512],
                                    c.swap_sb[:, :],
                                    qr[pr][:, half * 512:(half + 1) * 512], True, True)
                                   for half in range(TC // 512)]
                            p.mmgroup(rmm, reads=[c.swap_sb, qr[pr]], writes=[rp[pr]],
                                      tag="rot")
                            cs = cos_sb[:, tch * TC:(tch + 1) * TC]
                            sn = sin_sb[:, tch * TC:(tch + 1) * TC]
                            p.tt("dve", t1[pr][:, :], qr[pr][:, :], cs, ALU.mult, tag="rc")
                            p.tt("dve", t2[pr][:, :], rp[pr][:, 0:TC], sn, ALU.mult,
                                 tag="rs")
                            p.tt("dve", qro[pr][:, :], t1[pr][:, :], t2[pr][:, :],
                                 ALU.add, tag="radd")
                        else:
                            if o_glob < D:
                                bcol = c.bqk1s_sb[:, ob * 8 + ot: ob * 8 + ot + 1]
                                sc1 = SCALE1
                            else:
                                bcol = c.bqk1_sb[:, ob * 8 + ot: ob * 8 + ot + 1]
                                sc1 = 1.0
                            p.ts("dve", qro[pr][:, :], qp[:, 0:TC], sc1,
                                 ALU.mult, bcol, ALU.add, tag="qbias")
                        dst = qdst if o_glob < D else kdst
                        r0 = o_glob if o_glob < D else o_glob - D
                        p.dma("sp", dst[r0:r0 + 128, tch * TC:(tch + 1) * TC],
                              qro[pr][:, :], tag="qout")
                else:
                    vb0 = (ob - 4) * 1024
                    for tt in range(TC // 128):
                        vp = pp[tt % 2]
                        mms = []
                        if layer == 1:
                            for half in range(2):
                                mms.append((vp[:, half * 512:(half + 1) * 512],
                                            c.onesf_row[:, :],
                                            c.bv1_sb[0:1, vb0 + half * 512: vb0 + (half + 1) * 512],
                                            True, False))
                        st0 = layer == 0
                        for dt in range(DT):
                            for half in range(2):
                                mms.append((vp[:, half * 512:(half + 1) * 512],
                                            atile(tch % 2 if NT > 1 else 0, dt,
                                                  tt * 128, 128),
                                            wtile(ob % 2, dt, half * 512, 512),
                                            st0 and dt == 0, dt == DT - 1))
                        rds = [wcur, acur] + ([c.onesf_row, c.bv1_sb] if layer == 1 else [])
                        p.mmgroup(mms, reads=rds, writes=[vp], tag=f"vmm{ob}")
                        p.copy("act", vo[tt % 2][:, :], vp[:, :], tag="vevac")
                        p.dma("sp", vdst[tch * TC + tt * 128: tch * TC + (tt + 1) * 128,
                                         vb0:vb0 + 1024], vo[tt % 2][:, :], tag="vout")
    p.barrier(skip_prefix=skip_prefix)


def phase_attn0(nc, p, c):
    T = c.T
    NW = T // WIN
    KL = NP + WIN

    with ExitStack() as ctx:
        sb = lambda n, s, dt=F32: ctx.enter_context(nc.sbuf_tensor("a0" + n, s, dt))
        ps = lambda n, s, dt=F32: ctx.enter_context(nc.psum_tensor("a0" + n, s, dt))
        wout_sb = sb("wout", [128, DT * D], BF16)
        mask_sb = sb("mask", [128, 4 * KL], BF16)
        qw = [sb(f"q{i}", [128, NH * WIN], BF16) for i in range(2)]
        kw = [sb(f"k{i}", [128, NH * KL], BF16) for i in range(2)]
        vw = [sb(f"v{i}", [128, 4 * D], BF16) for i in range(2)]
        ao = sb("ao", [128, NH * WIN], BF16)
        pws = [sb(f"pws{i}", [128, KL], BF16) for i in range(2)]
        pwn = [sb(f"pwn{i}", [128, KL], BF16) for i in range(2)]
        ptw = sb("ptw", [128, 4 * WIN], BF16)
        ptp = sb("ptp", [16, WIN], BF16)
        wo = sb("wo", [128, WIN], BF16)
        r1 = [sb(f"r1{i}", [128, 1]) for i in range(2)]
        nm = [sb(f"nm{i}", [128, 1]) for i in range(2)]
        aW = [sb(f"aW{i}", [128, 1]) for i in range(2)]
        rec = [sb(f"rec{i}", [128, 1]) for i in range(2)]
        simp = [ps(f"sim{i}", [128, 1024]) for i in range(2)]
        trp = ps("trp", [128, 1024], BF16)
        ovp = ps("ovp", [128, 512])
        wop = [ps(f"wop{i}", [128, 512]) for i in range(2)]

        p.dma("sp", wout_sb[:, :].rearrange("p (dt o) -> p dt o", o=D),
              c.wout0T_bf, tag="woutld")
        p.dma("sp", mask_sb[:, :].rearrange("p (qs k) -> p qs k", k=KL),
              c.maskW.rearrange("qs p k -> p qs k"), tag="maskld")

        for w in range(NW):
            col = slice(w * WIN, (w + 1) * WIN)
            qc, kc, vc = qw[w % 2], kw[w % 2], vw[w % 2]
            p.dma("sp", qc[:, :].rearrange("p (h t) -> p h t", t=WIN),
                  c.q0T.rearrange("(h p) t -> p h t", p=128)[:, :, col], tag="qld")
            p.dma("sp", kc[:, :].rearrange("p (h t) -> p h t", t=KL)[:, :, 0:NP],
                  c.pmk_bf.rearrange("(h p) n -> p h n", p=128), tag="kpld")
            p.dma("sp", kc[:, :].rearrange("p (h t) -> p h t", t=KL)[:, :, NP:KL],
                  c.k0T.rearrange("(h p) t -> p h t", p=128)[:, :, col], tag="kld")
            p.dma("sp", vc[:, :].rearrange("p (kt d) -> p kt d", d=D),
                  c.v0.rearrange("(kt p) d -> p kt d", p=128)[:, w * 4:(w + 1) * 4],
                  tag="vld")
            for h in range(NH):
                for qs in range(4):
                    pr = qs % 2
                    sim = simp[pr]
                    qt = qc[:, h * WIN + qs * 128: h * WIN + (qs + 1) * 128]
                    p.mmgroup(
                        [(sim[:, 0:512], c.identb_sb[:, :],
                          mask_sb[:, qs * KL: qs * KL + 512], True, False),
                         (sim[:, 512:KL], c.identb_sb[:, :],
                          mask_sb[:, qs * KL + 512: (qs + 1) * KL], True, False),
                         (sim[:, 0:512], qt, kc[:, h * KL: h * KL + 512], False, True),
                         (sim[:, 512:KL], qt, kc[:, h * KL + 512: h * KL + KL],
                          False, True)],
                        reads=[qc, kc, c.identb_sb, mask_sb], writes=[sim], tag="simW")
                    p.red(nm[pr][:, :], sim[:, 0:KL], ALU.max, negate=True, tag="r1")
                    p.act(pws[pr][:, :], sim[:, 0:KL], AF.Exp, bias=nm[pr][:, :],
                          accum=aW[pr][:, :], tag="expW")
                    p.recip(rec[pr][:, :], aW[pr][:, :], tag="rec")
                    p.op("act", lambda e, pr=pr: e.activation(
                        pwn[pr][:, :], pws[pr][:, :], AF.Copy, scale=rec[pr][:, :]),
                        reads=[pws[pr], rec[pr]], writes=[pwn[pr]], tag="pw")
                    for kt in range(4):
                        p.transp(trp[:, kt * 128:(kt + 1) * 128],
                                 pwn[pr][:, NP + kt * 128: NP + (kt + 1) * 128],
                                 c.identb_sb[:, :], tag="ptr")
                    p.transp(trp[0:NP, 512:640], pwn[pr][:, 0:NP],
                             c.identb_sb[:, :], tag="ptrP")
                    p.copy("dve",
                           ptw[:, :].rearrange("p (kt q) -> p kt q", q=WIN)
                           [:, :, qs * 128:(qs + 1) * 128],
                           trp[:, 0:512].rearrange("p (kt q) -> p kt q", q=128),
                           tag="ptev")
                    p.copy("dve", ptp[0:NP, qs * 128:(qs + 1) * 128],
                           trp[0:NP, 512:640], tag="ptevP")
                mms = [(ovp[:, :], c.pm1_sb[0:NP, h * DH:(h + 1) * DH],
                        ptp[0:NP, :], True, False)]
                for kt in range(4):
                    mms.append((ovp[:, :], vc[:, kt * D + h * DH: kt * D + (h + 1) * DH],
                                ptw[:, kt * WIN:(kt + 1) * WIN], False, kt == 3))
                p.mmgroup(mms, reads=[c.pm1_sb, ptp, vc, ptw], writes=[ovp], tag="pv")
                p.copy("act", ao[:, h * WIN:(h + 1) * WIN], ovp[:, :], tag="aoev")
            for ot in range(DT):
                wp = wop[ot % 2]
                p.mmgroup([(wp[:, :], wout_sb[:, dt * D + ot * 128: dt * D + (ot + 1) * 128],
                            ao[:, dt * WIN:(dt + 1) * WIN], dt == 0, dt == DT - 1)
                           for dt in range(DT)],
                          reads=[wout_sb, ao], writes=[wp], tag="womm")
                p.copy("act", wo[:, :], wp[:, :], tag="woev")
                p.dma("pool", c.at0T[ot * 128:(ot + 1) * 128, col], wo[:, :], tag="woout")
    p.barrier()


def phase_moe0(nc, p, c, skip_prefix=None):
    T = c.T
    C = min(1024, T)

    with ExitStack() as ctx:
        sb = lambda n, s, dt=F32: ctx.enter_context(nc.sbuf_tensor("m0" + n, s, dt))
        ps = lambda n, s, dt=F32: ctx.enter_context(nc.psum_tensor("m0" + n, s, dt))
        xnbs = [sb(f"xn{i}", [128, DT * C], BF16) for i in range(2)]
        comb = sb("comb", [128, DT * C], BF16)
        ew = [sb(f"ew{i}", [128, DT * 512], BF16) for i in range(2)]
        silu = sb("silu", [128, C], BF16)
        tmp = sb("tmp", [128, C], BF16)
        wrow = sb("wrow", [1, C])
        wbc = sb("wbc", [128, C], BF16)
        yb = sb("y", [128, DT * C], BF16)
        rows = sb("rows", [1, C])
        x32t = sb("x32t", [128, 2 * C])
        at0t = sb("at0t", [128, 2 * C], BF16)
        hp = [ps(f"hp{i}", [128, 1024]) for i in range(2)]
        bcps = ps("bcps", [128, C])
        s2ps = ps("s2ps", [128, 1024])

        NCH = T // C
        for ch in range(NCH):
            col = slice(ch * C, (ch + 1) * C)
            xnb = xnbs[ch % 2]
            for h in range(2):
                sl = slice(h * 8, (h + 1) * 8)
                p.dma("sp", xnb[:, :].rearrange("p (dt t) -> p dt t", t=C)[:, sl],
                      c.xn0T.rearrange("(dt p) t -> p dt t", p=128)[:, sl, col],
                      tag="xnld")
            for e in range(E):
                p.dma("sp", wrow[:, :], c.w0rT[e:e + 1, col], tag="wrow")
                p.mmgroup([(bcps[:, 0:512], c.onesf_row[:, :], wrow[0:1, 0:512],
                            True, True)] +
                          ([(bcps[:, 512:1024], c.onesf_row[:, :], wrow[0:1, 512:1024],
                             True, True)] if C > 512 else []),
                          reads=[c.onesf_row, wrow], writes=[bcps], tag="wbc")
                p.copy("dve", wbc[:, :], bcps[:, 0:C], tag="wbccp")
                for fh in range(4):
                    ewc = ew[fh % 2]
                    p.dma("sp", ewc[:, :].rearrange("p (dt f) -> p dt f", f=512),
                          c.ew0_bf[e, fh], tag="ewld")
                    for ft8 in range(4):
                        ftg = fh * 4 + ft8
                        hpp = hp[ft8 % 2]
                        mms = []
                        for dt in range(DT):
                            for half in range(C // 512):
                                mms.append((hpp[:, half * 512:(half + 1) * 512],
                                            ewc[:, dt * 512 + ft8 * 128:
                                                dt * 512 + (ft8 + 1) * 128],
                                            xnb[:, dt * C + half * 512:
                                                dt * C + (half + 1) * 512],
                                            dt == 0, dt == DT - 1))
                        p.mmgroup(mms, reads=[ewc, xnb], writes=[hpp], tag="hmm")
                        p.act(silu[:, :], hpp[:, 0:C], AF.Silu,
                              bias=c.eb0_sb[:, e * DT + ftg: e * DT + ftg + 1],
                              tag="silu")
                        if e == 0:
                            p.tt("dve", comb[:, ftg * C:(ftg + 1) * C], silu[:, :],
                                 wbc[:, :], ALU.mult, tag="cmb0")
                        else:
                            p.tt("dve", tmp[:, :], silu[:, :], wbc[:, :], ALU.mult,
                                 tag="cmbm")
                            p.tt("dve", comb[:, ftg * C:(ftg + 1) * C],
                                 comb[:, ftg * C:(ftg + 1) * C], tmp[:, :], ALU.add,
                                 tag="cmba")
            for fh in range(4):
                ewc = ew[1]
                p.dma("sp", ewc[:, :].rearrange("p (dt f) -> p dt f", f=512),
                      c.pw0_bf[fh], tag="pwld")
                for ot8 in range(4):
                    otg = fh * 4 + ot8
                    hpp = hp[ot8 % 2]
                    mms = []
                    for dt in range(DT):
                        for half in range(C // 512):
                            mms.append((hpp[:, half * 512:(half + 1) * 512],
                                        ewc[:, dt * 512 + ot8 * 128:
                                            dt * 512 + (ot8 + 1) * 128],
                                        comb[:, dt * C + half * 512:
                                             dt * C + (half + 1) * 512],
                                        dt == 0, dt == DT - 1))
                    p.mmgroup(mms, reads=[ewc, comb], writes=[hpp], tag="pmm")
                    p.stt("dve", yb[:, otg * C:(otg + 1) * C], hpp[:, 0:C],
                          c.pb0_sb[:, otg:otg + 1], xnb[:, otg * C:(otg + 1) * C],
                          ALU.add, ALU.add, tag="y")
            for ot in range(DT):
                p.act(silu[:, :], yb[:, ot * C:(ot + 1) * C], AF.Square, tag="ysq")
                p.mmgroup([(s2ps[0:1, half * 512:(half + 1) * 512], c.onesb_col[:, :],
                            silu[:, half * 512:(half + 1) * 512],
                            ot == 0, ot == DT - 1) for half in range(C // 512)],
                          reads=[c.onesb_col, silu], writes=[s2ps], tag="ys")
            p.ts("dve", rows[0:1, 0:C], s2ps[0:1, 0:C], 1.0 / D, ALU.mult, tag="ym")
            p.act(rows[0:1, 0:C], rows[0:1, 0:C], AF.Ln,
                  bias=c.eps_rms[0:1, :], tag="yln")
            p.act(rows[0:1, 0:C], rows[0:1, 0:C], AF.Exp, scale=-0.5, tag="yex")
            p.mmgroup([(bcps[:, half * 512:(half + 1) * 512], c.onesf_row[:, :],
                        rows[0:1, half * 512:(half + 1) * 512], True, True)
                       for half in range(C // 512)],
                      reads=[c.onesf_row, rows], writes=[bcps], tag="ybc")
            for q2 in range(8):
                ots = range(q2 * 2, (q2 + 1) * 2)
                p.dma("sp", x32t[:, :].rearrange("p (o t) -> p o t", t=C),
                      c.xT.rearrange("(o p) t -> p o t", p=128)
                      [:, q2 * 2:(q2 + 1) * 2, col], tag="xld")
                p.dma("sp", at0t[:, :].rearrange("p (o t) -> p o t", t=C),
                      c.at0T.rearrange("(o p) t -> p o t", p=128)
                      [:, q2 * 2:(q2 + 1) * 2, col], tag="atld")
                p.tt("dve", x32t[:, :], x32t[:, :], at0t[:, :], ALU.add, tag="addat")
                for j, ot in enumerate(ots):
                    p.stt("dve", tmp[:, :], yb[:, ot * C:(ot + 1) * C],
                          c.fnw0_sb[:, ot:ot + 1], bcps[:, 0:C], ALU.mult, ALU.mult,
                          tag="ffn")
                    p.tt("dve", x32t[:, j * C:(j + 1) * C],
                         x32t[:, j * C:(j + 1) * C], tmp[:, :], ALU.add, tag="addx")
                p.dma("sp", c.x1T.rearrange("(o p) t -> p o t", p=128)
                      [:, q2 * 2:(q2 + 1) * 2, col],
                      x32t[:, :].rearrange("p (o t) -> p o t", t=C), tag="x1out")
    p.barrier(skip_prefix=skip_prefix)


def phase_attn1(nc, p, c):
    T = c.T
    QC = min(512, T)
    NQ = T // QC
    KB = min(1024, T)
    NKB = T // KB

    with ExitStack() as ctx:
        sb = lambda n, s, dt=F32: ctx.enter_context(nc.sbuf_tensor("a1" + n, s, dt))
        ps = lambda n, s, dt=F32: ctx.enter_context(nc.psum_tensor("a1" + n, s, dt))
        qc = sb("q", [128, DT * QC], BF16)
        kb = [sb(f"k{i}", [128, DT * KB], BF16) for i in range(2)]
        vb = [sb(f"v{i}", [128, (KB // 128) * D], BF16) for i in range(2)]
        pb = [sb(f"p{i}", [128, KB], BF16) for i in range(2)]
        ptb = [sb(f"pt{i}", [128, KB], BF16) for i in range(2)]
        ctxa = [sb(f"cx{i}", [128, D]) for i in range(QC // 128)]
        ctxb = sb("cb", [128, D], BF16)
        ctxt = sb("ct", [128, 128 * (DT // 2)], BF16)
        nmrun = sb("m", [128, QC // 128])
        arun = sb("a", [128, QC // 128])
        nblk = [sb(f"rb{i}", [128, 1]) for i in range(2)]
        ablk = [sb(f"ab{i}", [128, 1]) for i in range(2)]
        nmn = [sb(f"mn{i}", [128, 1]) for i in range(2)]
        corr = [sb(f"co{i}", [128, 1]) for i in range(2)]
        rec = sb("rc", [128, 1])
        simp = [ps(f"s{i}", [128, 1024]) for i in range(2)]
        pvp = [ps(f"pv{i}", [128, 512]) for i in range(2)]
        trp = ps("tr", [128, 1024], BF16)
        trc = ps("tc", [128, 512], BF16)

        for qch in range(NQ):
            qcol = slice(qch * QC, (qch + 1) * QC)
            for h in range(2):
                sl = slice(h * 8, (h + 1) * 8)
                p.dma("sp", qc[:, :].rearrange("p (dt t) -> p dt t", t=QC)[:, sl],
                      c.q1T.rearrange("(dt p) t -> p dt t", p=128)[:, sl, qcol],
                      tag="q1ld")
            for kbi in range(NKB):
                kcur, vcur = kb[kbi % 2], vb[kbi % 2]
                for h in range(2):
                    sl = slice(h * 8, (h + 1) * 8)
                    p.dma("sp", kcur[:, :].rearrange("p (dt t) -> p dt t", t=KB)[:, sl],
                          c.k1T.rearrange("(dt p) t -> p dt t", p=128)
                          [:, sl, kbi * KB:(kbi + 1) * KB], tag="k1ld")
                p.dma("sp", vcur[:, :].rearrange("p (kt d) -> p kt d", d=D),
                      c.v1.rearrange("(kt p) d -> p kt d", p=128)
                      [:, kbi * (KB // 128):(kbi + 1) * (KB // 128)], tag="v1ld")
                for qs in range(QC // 128):
                    pr = qs % 2
                    sp_ = simp[pr]
                    mms = []
                    for half in range(KB // 512):
                        for dt in range(DT):
                            mms.append((sp_[:, half * 512:(half + 1) * 512],
                                        qc[:, dt * QC + qs * 128: dt * QC + (qs + 1) * 128],
                                        kcur[:, dt * KB + half * 512:
                                             dt * KB + (half + 1) * 512],
                                        dt == 0, dt == DT - 1))
                    p.mmgroup(mms, reads=[qc, kcur], writes=[sp_], tag="sim1")
                    nmr = nmrun[:, qs:qs + 1]
                    ar = arun[:, qs:qs + 1]
                    p.red(nblk[pr][:, :], sp_[:, 0:KB], ALU.max, negate=True, tag="rb")
                    if kbi == 0:
                        p.copy("dve", nmr, nblk[pr][:, :], tag="m0")
                        p.act(pb[pr][:, :], sp_[:, 0:KB], AF.Exp, bias=nblk[pr][:, :],
                              accum=ablk[pr][:, :], tag="exp1")
                        p.copy("dve", ar, ablk[pr][:, :], tag="a0")
                    else:
                        p.tt("dve", nmn[pr][:, :], nmr, nblk[pr][:, :], ALU.min,
                             tag="mn")
                        p.act(corr[pr][:, :], nmr, AF.Exp, bias=nmn[pr][:, :],
                              scale=-1.0, tag="corr")
                        p.copy("dve", nmr, nmn[pr][:, :], tag="mupd")
                        p.act(pb[pr][:, :], sp_[:, 0:KB], AF.Exp, bias=nmn[pr][:, :],
                              accum=ablk[pr][:, :], tag="exp1")
                        p.stt("dve", ar, ar, corr[pr][:, :], ablk[pr][:, :], ALU.mult,
                              ALU.add, tag="aupd")
                    for kt in range(KB // 128):
                        p.transp(trp[:, kt * 128:(kt + 1) * 128],
                                 pb[pr][:, kt * 128:(kt + 1) * 128],
                                 c.identb_sb[:, :], tag="ptr1")
                    p.copy("dve", ptb[pr][:, :], trp[:, 0:KB], tag="ptev1")
                    for dc in range(4):
                        pv = pvp[dc % 2]
                        p.mmgroup([(pv[:, :], ptb[pr][:, kt * 128:(kt + 1) * 128],
                                    vcur[:, kt * D + dc * 512: kt * D + (dc + 1) * 512],
                                    kt == 0, kt == KB // 128 - 1)
                                   for kt in range(KB // 128)],
                                  reads=[ptb[pr], vcur], writes=[pv], tag="pv1")
                        cx = ctxa[qs][:, dc * 512:(dc + 1) * 512]
                        if kbi == 0:
                            p.copy("dve", cx, pv[:, :], tag="cx0")
                        else:
                            p.stt("dve", cx, cx, corr[pr][:, :], pv[:, :], ALU.mult,
                                  ALU.add, tag="cxu")
            for qs in range(QC // 128):
                p.recip(rec[:, :], arun[:, qs:qs + 1], tag="rec1")
                p.op("act", lambda e, qs=qs: e.activation(
                    ctxb[:, :], ctxa[qs][:, :], AF.Copy, scale=rec[:, :]),
                    reads=[ctxa[qs], rec], writes=[ctxb], tag="cnrm")
                for dh in range(2):
                    for j in range(DT // 2):
                        dt = dh * (DT // 2) + j
                        sl8 = j % 4
                        p.transp(trc[:, sl8 * 128:(sl8 + 1) * 128],
                                 ctxb[:, dt * 128:(dt + 1) * 128],
                                 c.identb_sb[:, :], tag="ctr")
                        p.copy("dve", ctxt[:, j * 128:(j + 1) * 128],
                               trc[:, sl8 * 128:(sl8 + 1) * 128], tag="ctev")
                    p.dma("pool", c.ctx1T.rearrange("(dt p) t -> p dt t", p=128)
                          [:, dh * (DT // 2):(dh + 1) * (DT // 2),
                           qch * QC + qs * 128: qch * QC + (qs + 1) * 128],
                          ctxt[:, :].rearrange("p (dt q) -> p dt q", q=128),
                          tag="ctxout")
    p.barrier()


def phase_wout1(nc, p, c):
    T = c.T
    TC = min(1024, T)
    with ExitStack() as ctx:
        sb = lambda n, s, dt=F32: ctx.enter_context(nc.sbuf_tensor("w1" + n, s, dt))
        ps = lambda n, s: ctx.enter_context(nc.psum_tensor("w1" + n, s, F32))
        wsb = sb("w", [128, DT * D], BF16)
        cb = [sb(f"c{i}", [128, DT * TC], BF16) for i in range(2)]
        ob = sb("o", [128, TC], BF16)
        wp = [ps(f"p{i}", [128, 512]) for i in range(4)]
        p.dma("sp", wsb[:, :].rearrange("p (dt o) -> p dt o", o=D),
              c.wout1T_bf, tag="w1ld")
        for tch in range(T // TC):
            col = slice(tch * TC, (tch + 1) * TC)
            cc = cb[tch % 2]
            for h in range(2):
                sl = slice(h * 8, (h + 1) * 8)
                p.dma("sp", cc[:, :].rearrange("p (dt t) -> p dt t", t=TC)[:, sl],
                      c.ctx1T.rearrange("(dt p) t -> p dt t", p=128)[:, sl, col],
                      tag="c1ld")
            for ot in range(DT):
                nh = TC // 512
                pp = [wp[(ot * nh + i) % 4] for i in range(nh)]
                mms = []
                for dt in range(DT):
                    for half in range(nh):
                        mms.append((pp[half][:, :],
                                    wsb[:, dt * D + ot * 128: dt * D + (ot + 1) * 128],
                                    cc[:, dt * TC + half * 512: dt * TC + (half + 1) * 512],
                                    dt == 0, dt == DT - 1))
                p.mmgroup(mms, reads=[wsb, cc], writes=pp, tag="w1mm")
                for half in range(nh):
                    p.op("act", lambda e, pp=pp, half=half, ot=ot:
                         e.add(ob[:, half * 512:(half + 1) * 512], pp[half][:, :],
                               c.bo1_sb[:, ot:ot + 1]),
                         reads=[pp[half], c.bo1_sb], writes=[ob], tag="w1b")
                p.dma("pool", c.at1T[ot * 128:(ot + 1) * 128, col], ob[:, 0:TC],
                      tag="w1out")
    p.barrier()


def phase_moe1(nc, p, c):
    T = c.T
    C = min(1024, T)
    NCH = T // C
    HB = 2

    with ExitStack() as ctx:
        sb = lambda n, s, dt=F32: ctx.enter_context(nc.sbuf_tensor("m1" + n, s, dt))
        ps = lambda n, s, dt=F32: ctx.enter_context(nc.psum_tensor("m1" + n, s, dt))
        xnb = sb("xn", [128, DT * C], BF16)
        outb = sb("out", [128, DT * C], BF16)
        hsb = sb("h", [128, HT * C], BF16)
        w13 = [sb(f"w13_{i}", [128, DT * HB * 128 * 2], BF16) for i in range(2)]
        w2sb = [sb(f"w2_{i}", [128, 7 * 1024], BF16) for i in range(2)]
        sbc = sb("sbc", [128, C], BF16)
        srow = sb("srow", [1, C])
        t1 = sb("t1", [128, C])
        t3 = sb("t3", [128, C], BF16)
        ab = sb("A", [128, C], BF16)
        x1t = sb("x1t", [128, 4 * C])
        at1t = sb("at1t", [128, 4 * C], BF16)
        u1p = ps("u1", [128, 1024])
        u3p = ps("u3", [128, 1024])
        op2 = [ps(f"o{i}", [128, 1024]) for i in range(2)]

        for ch in range(NCH):
            col = slice(ch * C, (ch + 1) * C)
            for h in range(2):
                sl = slice(h * 8, (h + 1) * 8)
                p.dma("sp", xnb[:, :].rearrange("p (dt t) -> p dt t", t=C)[:, sl],
                      c.xn1T.rearrange("(dt p) t -> p dt t", p=128)[:, sl, col],
                      tag="xn1ld")
            for ei in range(E + 1):
                routed = ei > 0
                e = ei - 1
                w2src = c.sw2_bf if not routed else c.w21_bf[e]
                if routed:
                    p.dma("sp", srow[:, :], c.s1rT[e:e + 1, col], tag="srow")
                    p.mmgroup([(op2[0][:, half * 512:(half + 1) * 512],
                                c.onesf_row[:, :],
                                srow[0:1, half * 512:(half + 1) * 512], True, True)
                               for half in range(C // 512)],
                              reads=[c.onesf_row, srow], writes=[op2[0]], tag="sbcmm")
                    p.copy("dve", sbc[:, :], op2[0][:, 0:C], tag="sbccp")
                for hb0 in range(0, HT, HB):
                    wcur = w13[(hb0 // HB) % 2]
                    w1src = c.sw1_bf if not routed else c.w11_bf[e]
                    w3src = c.sw3_bf if not routed else c.w31_bf[e]
                    hb = hb0 // HB
                    p.dma("sp",
                          wcur[:, :].rearrange("p (k dt h) -> p k dt h",
                                               k=2, h=HB * 128)[:, 0],
                          w1src[hb], tag="w1ld")
                    p.dma("sp",
                          wcur[:, :].rearrange("p (k dt h) -> p k dt h",
                                               k=2, h=HB * 128)[:, 1],
                          w3src[hb], tag="w3ld")
                    for hi in range(HB):
                        ht = hb0 + hi
                        mms1, mms3 = [], []
                        for dt in range(DT):
                            for half in range(C // 512):
                                lh1 = wcur[:, dt * HB * 128 + hi * 128:
                                           dt * HB * 128 + (hi + 1) * 128]
                                lh3 = wcur[:, DT * HB * 128 + dt * HB * 128 + hi * 128:
                                           DT * HB * 128 + dt * HB * 128 + (hi + 1) * 128]
                                rh = xnb[:, dt * C + half * 512: dt * C + (half + 1) * 512]
                                mms1.append((u1p[:, half * 512:(half + 1) * 512],
                                             lh1, rh, dt == 0, dt == DT - 1))
                                mms3.append((u3p[:, half * 512:(half + 1) * 512],
                                             lh3, rh, dt == 0, dt == DT - 1))
                        p.mmgroup(mms1, reads=[wcur, xnb], writes=[u1p], tag="u1mm")
                        p.mmgroup(mms3, reads=[wcur, xnb], writes=[u3p], tag="u3mm")
                        if routed:
                            p.tt("dve", t1[:, :], u1p[:, 0:C], sbc[:, :], ALU.mult,
                                 tag="t1")
                            p.act(ab[:, :], t1[:, :], AF.Silu, tag="sl1")
                            p.tt("dve", t3[:, :], u3p[:, 0:C], sbc[:, :], ALU.mult,
                                 tag="t3")
                            p.tt("dve", hsb[:, ht * C:(ht + 1) * C], ab[:, :], t3[:, :],
                                 ALU.mult, tag="h")
                        else:
                            p.act(ab[:, :], u1p[:, 0:C], AF.Silu, tag="sl1s")
                            p.tt("dve", hsb[:, ht * C:(ht + 1) * C], ab[:, :],
                                 u3p[:, 0:C], ALU.mult, tag="hs")
                for blk4 in range(4):
                    w2h, dh = blk4 // 2, blk4 % 2
                    wc = w2sb[blk4 % 2]
                    p.dma("sp", wc[:, :].rearrange("p (ht o) -> p ht o", o=1024),
                          w2src[w2h][:, :, dh * 1024:(dh + 1) * 1024], tag="w2ld")
                    for ot8 in range(8):
                        ot = dh * 8 + ot8
                        pp = op2[ot8 % 2]
                        mms = []
                        for j in range(7):
                            ht = w2h * 7 + j
                            for half in range(C // 512):
                                mms.append((pp[:, half * 512:(half + 1) * 512],
                                            wc[:, j * 1024 + ot8 * 128:
                                               j * 1024 + (ot8 + 1) * 128],
                                            hsb[:, ht * C + half * 512:
                                                ht * C + (half + 1) * 512],
                                            j == 0, j == 6))
                        p.mmgroup(mms, reads=[wc, hsb], writes=[pp], tag="w2mm")
                        osl = outb[:, ot * C:(ot + 1) * C]
                        if ei == 0 and w2h == 0:
                            p.copy("dve", osl, pp[:, 0:C], tag="ocp")
                        else:
                            p.tt("dve", osl, osl, pp[:, 0:C], ALU.add, tag="oadd")
            for q4 in range(4):
                p.dma("sp", x1t[:, :].rearrange("p (o t) -> p o t", t=C),
                      c.x1T.rearrange("(o p) t -> p o t", p=128)
                      [:, q4 * 4:(q4 + 1) * 4, col], tag="x1ld")
                p.dma("sp", at1t[:, :].rearrange("p (o t) -> p o t", t=C),
                      c.at1T.rearrange("(o p) t -> p o t", p=128)
                      [:, q4 * 4:(q4 + 1) * 4, col], tag="at1ld")
                p.tt("dve", x1t[:, :], x1t[:, :], at1t[:, :], ALU.add, tag="fa")
                p.tt("dve", x1t[:, :], x1t[:, :],
                     outb[:, q4 * 4 * C:(q4 + 1) * 4 * C], ALU.add, tag="fx")
                p.dma("pool", c.outT.rearrange("(o p) t -> p o t", p=128)
                      [:, q4 * 4:(q4 + 1) * 4, col],
                      x1t[:, :].rearrange("p (o t) -> p o t", t=C), tag="oout")
    p.barrier()
